# version 19
# baseline (speedup 1.0000x reference)
import numpy as np
import concourse.bass as bass
import concourse.mybir as mybir
from concourse.bass_utils import run_bass_kernel_spmd

F32 = mybir.dt.float32
BF16 = mybir.dt.bfloat16
FP8 = mybir.dt.float8e4
AF = mybir.ActivationFunctionType
ALU = mybir.AluOpType
AX = mybir.AxisListType

T = 8192
D = 1024
DC = 8
PT_A = 512
NT_A = T // PT_A
TQ = 256
NQT = T // TQ
SUP = 512


class Ev:
    __slots__ = ("sem", "val")

    def __init__(self, sem, val):
        self.sem = sem
        self.val = val


class Eng:
    def __init__(self, nc, block_name, self_sync, n_dma_lanes=0):
        self.block_name = block_name
        self.items = []
        self.sem = nc.alloc_semaphore(f"sm_{block_name}")
        self.count = 0
        self.self_sync = self_sync
        self._waited = {}
        self.dma_sems = [nc.alloc_semaphore(f"dm_{block_name}{i}") for i in range(n_dma_lanes)]
        self.dma_counts = [0] * n_dma_lanes
        self.dma_idx = 0

    def wait(self, ev):
        if ev is None:
            return
        key = ev.sem.num
        if self._waited.get(key, -1) >= ev.val:
            return
        self._waited[key] = ev.val
        self.items.append(("w", ev.sem, ev.val))

    def op(self, fn, dma=False, inc=True):
        if self.self_sync and self.count > 0 and not dma:
            self.wait(Ev(self.sem, self.count))
        if not inc and not dma:
            self.items.append(("n", fn))
            return None
        if dma and self.dma_sems:
            i = self.dma_idx % len(self.dma_sems)
            self.dma_idx += 1
            self.wait(Ev(self.dma_sems[i], self.dma_counts[i]))
            self.items.append(("o", fn, self.dma_sems[i], 16))
            self.dma_counts[i] += 16
            return Ev(self.dma_sems[i], self.dma_counts[i])
        amt = 16 if dma else 1
        self.items.append(("o", fn, self.sem, amt))
        self.count += amt
        return Ev(self.sem, self.count)

    def raw(self, fn):
        self.items.append(("r", fn))

    def _replay(self, raw_eng):
        for it in self.items:
            if it[0] == "w":
                raw_eng.wait_ge(it[1], it[2])
            elif it[0] == "o":
                it[1](raw_eng).then_inc(it[2], it[3])
            elif it[0] == "n":
                it[1](raw_eng)
            else:
                it[1](raw_eng)


class Sched:
    def __init__(self, nc):
        self.nc = nc
        self.engines = []

    def eng(self, block_name, self_sync=False, n_dma_lanes=0):
        e = Eng(self.nc, block_name, self_sync, n_dma_lanes)
        self.engines.append(e)
        return e

    def emit(self):
        with self.nc.Block() as block:
            for e in self.engines:
                getattr(block, e.block_name)(e._replay)


def build_program(debug_scratch=False):
    nc = bass.Bass("TRN2", target_bir_lowering=False)
    scratch_kind = "ExternalOutput" if debug_scratch else "Internal"

    x_h = nc.dram_tensor("L0_attn_y_all", [T, D], F32, kind="ExternalInput").ap()
    wq_h = nc.dram_tensor("L0_attn_Wq", [D, D], F32, kind="ExternalInput").ap()
    wk_h = nc.dram_tensor("L0_attn_Wk", [D, D], F32, kind="ExternalInput").ap()
    wv_h = nc.dram_tensor("L0_attn_Wv", [D, D], F32, kind="ExternalInput").ap()
    wo_h = nc.dram_tensor("L0_attn_Wo", [D, D], F32, kind="ExternalInput").ap()
    bqd_h = nc.dram_tensor("L0_attn_bq", [128, DC], F32, kind="ExternalInput").ap()
    bkd_h = nc.dram_tensor("L0_attn_bk", [128, DC], F32, kind="ExternalInput").ap()
    bvr_h = nc.dram_tensor("L0_attn_bv", [1, D], F32, kind="ExternalInput").ap()
    bor_h = nc.dram_tensor("L0_attn_bo", [1, D], F32, kind="ExternalInput").ap()
    w1_h = nc.dram_tensor("L0_ffn_kernel", [D, D], BF16, kind="ExternalInput").ap()
    sin_h = nc.dram_tensor("L0_ffn_input_scale", [1, 1], F32, kind="ExternalInput").ap()
    skl_h = nc.dram_tensor("L0_ffn_kernel_scale", [1, 1], F32, kind="ExternalInput").ap()
    fb_h = nc.dram_tensor("L0_ffn_bias", [1, D], BF16, kind="ExternalInput").ap()
    out_h = nc.dram_tensor("out", [T, D], F32, kind="ExternalOutput").ap()

    x1d = nc.dram_tensor("x1d", [T, D], F32, kind=scratch_kind).ap()
    qd = nc.dram_tensor("qd", [DC, 128, T], BF16, kind=scratch_kind).ap()
    kd = nc.dram_tensor("kd", [DC, 128, T], BF16, kind=scratch_kind).ap()
    vt = nc.dram_tensor("vt", [T, D], BF16, kind=scratch_kind).ap()

    def sb(name, shape, dt):
        return nc.alloc_sbuf_tensor(name, shape, dt)

    wq_sb = sb("wq_sb", [128, DC * D], BF16)
    wk_sb = sb("wk_sb", [128, DC * D], BF16)
    wv_sb = sb("wv_sb", [128, DC * D], BF16)
    wo_sb = sb("wo_sb", [128, DC * D], BF16)
    w8_sb = sb("w8_sb", [128, DC * D], FP8)
    bqd = sb("bqd", [128, DC], F32)
    bkd = sb("bkd", [128, DC], F32)
    bv_bc = sb("bv_bc", [128, D], F32)
    bo_bc = sb("bo_bc", [128, D], F32)
    fb_bc = sb("fb_bc", [128, D], BF16)
    sin11 = sb("sin11", [1, 1], F32)
    skl11 = sb("skl11", [1, 1], F32)
    sinv11 = sb("sinv11", [1, 1], F32)
    skinv11 = sb("skinv11", [1, 1], F32)
    pss11 = sb("pss11", [1, 1], F32)
    sinv_c = sb("sinv_c", [128, 1], F32)
    skinv_c = sb("skinv_c", [128, 1], F32)
    pss_c = sb("pss_c", [128, 1], F32)
    ident = sb("ident", [128, 128], BF16)
    maskA = sb("maskA", [128, TQ], BF16)
    maskB = sb("maskB", [128, TQ], BF16)
    ones_col16 = sb("ones_col16", [128, 1], BF16)
    ones_row32 = sb("ones_row32", [1, 128], F32)
    ones_row16 = sb("ones_row16", [1, 128], BF16)
    one11 = sb("one11", [1, 1], F32)
    kmax2 = sb("kmax2", [1, 1], F32)
    lnk2h = sb("lnk2h", [1, 1], F32)
    tmp11 = sb("tmp11", [1, 1], F32)
    eps_c = sb("eps_c", [128, 1], F32)

    xin = [sb(f"xin{i}", [128, 4096], F32) for i in range(2)]
    x1s = [sb(f"x1s{i}", [128, 4096], F32) for i in range(2)]
    qdm = sb("qdm", [128, 4096], BF16)
    kdm = sb("kdm", [128, 4096], BF16)
    vtm = sb("vtm", [128, 4096], BF16)
    y16T = sb("y16T", [128, 4096], BF16)
    q8T = sb("q8T", [128, 2048], FP8)
    x16a = [sb(f"x16a_{i}", [128, 1024], BF16) for i in range(2)]
    x16b = [sb(f"x16b_{i}", [128, 1024], BF16) for i in range(2)]
    f16a = sb("f16a", [128, 1024], BF16)
    statsA = [sb(f"statsA{i}", [128, 8], F32) for i in range(2)]
    statsB = [sb(f"statsB{i}", [128, 8], F32) for i in range(2)]
    bnbA = [sb(f"bnbA{i}", [128, 12], F32) for i in range(2)]
    bnbB = [sb(f"bnbB{i}", [128, 12], F32) for i in range(2)]
    ksq = [sb(f"ksq{i}", [128, 512], BF16) for i in range(2)]
    lnqA = sb("lnqA", [1, 512], F32)
    mall = sb("mall", [1, 32], F32)
    tmpM = sb("tmpM", [1, 1], F32)
    ptt = sb("ptt", [128, TQ], BF16)
    pt = [sb(f"pt{i}", [128, TQ], BF16) for i in range(4)]
    lnq = sb("lnq", [1, TQ], F32)
    negm = sb("negm", [1, 1], F32)
    negmcol = [sb(f"negmcol{i}", [128, 1], F32) for i in range(2)]
    rrow = sb("rrow", [1, TQ], F32)
    rcol = [sb(f"rcol{i}", [128, 1], F32) for i in range(2)]

    kst = [xin[0].ap().bitcast(BF16)[:, 0:4096], xin[1].ap().bitcast(BF16)[:, 0:4096],
           wq_sb[:, 0:4096], wk_sb[:, 0:4096]]
    vst = [xin[0].ap().bitcast(BF16)[:, 4096:8192], xin[1].ap().bitcast(BF16)[:, 4096:8192],
           wq_sb[:, 4096:8192], wk_sb[:, 4096:8192]]
    qtl = [qdm[:, i * 2048:(i + 1) * 2048] for i in range(2)]
    qsqB = kdm[:, 0:2048]
    ctx16 = [kdm[:, 2048 + u * 1024: 2048 + (u + 1) * 1024] for u in range(2)]
    ctxT = [y16T[:, u * 1024:(u + 1) * 1024] for u in range(2)]
    x2f = [vtm.ap().bitcast(F32)[:, u * 1024:(u + 1) * 1024] for u in range(2)]
    x1in = [x1s[i][:, 0:2048] for i in range(2)]
    outs = [x1s[i][:, 2048:4096] for i in range(2)]
    bvr = x1s[0][0:1, 0:1024]
    bor = x1s[0][0:1, 1024:2048]
    fbr = x1s[0].ap().bitcast(BF16)[0:1, 4096:5120]

    ctxp = nc.alloc_psum_tensor("ctxp", [128, 2048], F32)
    stp = [nc.alloc_psum_tensor(f"stp{i}", [128, 512], F32) for i in range(2)]
    sump = nc.alloc_psum_tensor("sump", [1, 512], F32)
    miscp = nc.alloc_psum_tensor("miscp", [128, 512], F32)
    miscp16 = miscp.ap().bitcast(BF16)

    s = Sched(nc)
    LD = s.eng("sync", n_dma_lanes=12)
    PE = s.eng("tensor")
    ACT = s.eng("scalar", self_sync=True)
    DVE = s.eng("vector", self_sync=True)
    STQ = s.eng("gpsimd", n_dma_lanes=8)

    ev_smalls = []
    for tgt, srch in [(bqd, bqd_h), (bkd, bkd_h), (sin11, sin_h), (skl11, skl_h)]:
        ev_smalls.append(LD.op(lambda e, t=tgt, h=srch: e.dma_start(out=t[:], in_=h[:]), dma=True))

    ev_ms = []
    ev_ms.append(STQ.op(lambda e: e.memset(ident[:], 0.0)))
    ev_ms.append(STQ.op(lambda e: e.memset(maskA[:], 1.0)))
    ev_ms.append(STQ.op(lambda e: e.memset(maskB[:], 1.0)))
    STQ.wait(ev_ms[2])
    STQ.op(lambda e: e.affine_select(out=ident[:], in_=ident[:],
                                     compare_op=ALU.not_equal, fill=1.0,
                                     base=0, pattern=[[-1, 128]], channel_multiplier=1))
    STQ.op(lambda e: e.affine_select(out=maskA[:], in_=maskA[:],
                                     compare_op=ALU.is_ge, fill=0.0,
                                     base=0, pattern=[[1, TQ]], channel_multiplier=-1))
    ev_consts = STQ.op(lambda e: e.affine_select(out=maskB[:], in_=maskB[:],
                                                 compare_op=ALU.is_ge, fill=0.0,
                                                 base=-128, pattern=[[1, TQ]], channel_multiplier=-1))
    ev_ones = None
    for t, v in [(ones_col16, 1.0), (ones_row32, 1.0), (ones_row16, 1.0),
                 (one11, 1.0), (kmax2, 0.0), (eps_c, 1e-5)]:
        ev_ones = DVE.op(lambda e, t=t, v=v: e.memset(t[:], v))

    stage_free = {("xin", 0): None, ("xin", 1): None, ("x1s", 0): None, ("x1s", 1): None}

    def plan_weight(w_h, w_sb, slab_name, slab):
        ev_last = None
        for half in range(2):
            st = slab[half]
            LD.wait(stage_free[(slab_name, half)])
            ev_ld = LD.op(lambda e, h=half, st=st, w_h=w_h: e.dma_start(
                out=st[:], in_=w_h.rearrange("(c p) d -> p c d", p=128)[:, 4 * h:4 * (h + 1), :]),
                dma=True)
            DVE.wait(ev_ld)
            ev_last = DVE.op(lambda e, h=half, st=st, w_sb=w_sb: e.tensor_copy(
                w_sb[:, h * 4096:(h + 1) * 4096], st[:]))
            stage_free[(slab_name, half)] = ev_last
        return ev_last

    plan_weight(wq_h, wq_sb, "xin", xin)
    ev_wk = plan_weight(wk_h, wk_sb, "xin", xin)
    plan_weight(wv_h, wv_sb, "x1s", x1s)
    ev_wo = plan_weight(wo_h, wo_sb, "x1s", x1s)

    for _e in ev_smalls:
        DVE.wait(_e)
    DVE.op(lambda e: e.reciprocal(sinv11[:], sin11[:]))
    DVE.op(lambda e: e.reciprocal(skinv11[:], skl11[:]))
    ev_pss = DVE.op(lambda e: e.tensor_tensor(out=pss11[:], in0=sin11[:], in1=skl11[:], op=ALU.mult))
    PE.wait(ev_pss)
    PE.wait(ev_ones)
    PE.op(lambda e: e.matmul(miscp[:, 0:1], ones_row32[:], sinv11[:], start=True, stop=True))
    PE.op(lambda e: e.matmul(miscp[:, 1:2], ones_row32[:], skinv11[:], start=True, stop=True))
    bc2 = PE.op(lambda e: e.matmul(miscp[:, 2:3], ones_row32[:], pss11[:], start=True, stop=True))
    DVE.wait(bc2)
    ev_sc = None
    for tgt, col in [(sinv_c, 0), (skinv_c, 1), (pss_c, 2)]:
        ev_sc = DVE.op(lambda e, t=tgt, c=col: e.tensor_copy(t[:], miscp[:, c:c + 1]))

    LD.wait(ev_wo)
    ev_r1 = LD.op(lambda e: e.dma_start(out=bvr, in_=bvr_h[:]), dma=True)
    ev_r2 = LD.op(lambda e: e.dma_start(out=bor, in_=bor_h[:]), dma=True)
    ev_r3 = LD.op(lambda e: e.dma_start(out=fbr, in_=fb_h[:]), dma=True)
    ev_bias_last = None
    PE.wait(ev_r1)
    PE.wait(ev_r2)
    PE.wait(ev_r3)
    PE.wait(ev_sc)
    for (row, bc, o16) in [(bvr, bv_bc, False), (bor, bo_bc, False), (fbr, fb_bc, True)]:
        for half in range(2):
            lhs = ones_row16 if o16 else ones_row32
            pv = PE.op(lambda e, row=row, h=half, lhs=lhs: e.matmul(
                miscp[:, 0:512], lhs[:], row[:, h * 512:(h + 1) * 512], start=True, stop=True))
            if o16:
                ACT.wait(pv)
                ev = ACT.op(lambda e, bc=bc, h=half: e.activation(
                    out=bc[:, h * 512:(h + 1) * 512], in_=miscp[:, 0:512], func=AF.Copy))
            else:
                DVE.wait(pv)
                ev = DVE.op(lambda e, bc=bc, h=half: e.tensor_copy(
                    bc[:, h * 512:(h + 1) * 512], miscp[:, 0:512]))
            PE.wait(ev)
            ev_bias_last = ev
    ev_x1s0_free = ev_bias_last

    ev_bq32 = DVE.op(lambda e: e.tensor_scalar(out=bqd[:], in0=bqd[:], scalar1=1.0 / 32.0,
                                               scalar2=None, op0=ALU.mult))

    ev_w8 = None
    for half in range(2):
        stb = xin[0].ap().bitcast(BF16)[:, 0:4096]
        LD.wait(ev_wk)
        LD.wait(ev_w8)
        ev_ld = LD.op(lambda e, h=half, stb=stb: e.dma_start(
            out=stb, in_=w1_h.rearrange("(c p) d -> p c d", p=128)[:, 4 * h:4 * (h + 1), :]),
            dma=True)
        ACT.wait(ev_ld)
        ACT.wait(ev_sc)
        ev_w8 = ACT.op(lambda e, h=half, stb=stb: e.activation(
            out=w8_sb[:, h * 4096:(h + 1) * 4096], in_=stb, func=AF.Identity,
            scale=skinv_c[:]))

    def plan_ln(src, dst16, bnb, stt, ev_src, ev_stats_free, ev_dst_free,
                apply_act=False):
        DVE.wait(ev_src)
        DVE.wait(ev_stats_free)
        DVE.op(lambda e, src=src, bnb=bnb: e.bn_stats(bnb[:, 0:6], src[:, 0:512]))
        DVE.op(lambda e, src=src, bnb=bnb: e.bn_stats(bnb[:, 6:12], src[:, 512:1024]))
        ev_aggr = DVE.op(lambda e, bnb=bnb, stt=stt: e.bn_aggr(stt[:, 0:2], bnb[:]))
        ACT.wait(ev_aggr)
        ACT.op(lambda e, stt=stt: e.activation(out=stt[:, 2:3], in_=stt[:, 1:2],
                                               func=AF.Ln, bias=eps_c[:]))
        ev_rstd = ACT.op(lambda e, stt=stt: e.activation(out=stt[:, 3:4], in_=stt[:, 2:3],
                                                         func=AF.Exp, scale=-0.5))
        DVE.wait(ev_rstd)
        ev_mr = DVE.op(lambda e, stt=stt: e.tensor_scalar(out=stt[:, 4:5], in0=stt[:, 0:1],
                                                          scalar1=stt[:, 3:4], scalar2=-1.0,
                                                          op0=ALU.mult, op1=ALU.mult))
        if apply_act:
            ACT.wait(ev_mr)
            ACT.wait(ev_dst_free)
            ev_dst = ACT.op(lambda e, src=src, dst16=dst16, stt=stt: e.activation(
                out=dst16[:], in_=src[:], func=AF.Identity,
                scale=stt[:, 3:4], bias=stt[:, 4:5]))
        else:
            DVE.wait(ev_dst_free)
            ev_dst = DVE.op(lambda e, src=src, dst16=dst16, stt=stt: e.tensor_scalar(
                out=dst16[:], in0=src[:], scalar1=stt[:, 3:4], scalar2=stt[:, 4:5],
                op0=ALU.mult, op1=ALU.add))
        return ev_dst

    ev_xin_free = [ev_w8, ev_wk]
    ev_x1s_free = [ev_x1s0_free, ev_wo]
    ev_qdm_free = None
    ev_kdm_free = None
    ev_x16a_free = [None, None]
    ev_x16b_free = [None, None]
    ev_stA_free = [None, None]
    ev_stB_free = [None, None]
    ev_misc_free2 = [ev_bias_last]
    ev_misc_free3 = [ev_bias_last]
    ev_x16a_free2 = [None, None]
    ev_stp_free = [None, None]
    ev_ctxpA_free = [None, None, None, None]
    ev_y16T_free = [None]
    ev_q8T_free = [None, None]
    ev_consts_waited = [False]
    ev_yslice_free = [None, None, None, None]
    knst = {"prev": None, "kmax": ev_ones, "qdm": None, "kdm": None,
            "vtm": None, "knmax": None, "lnq": None, "mall": [None] * 32}

    def wait_all(E, ev):
        if isinstance(ev, list):
            for _x in ev:
                E.wait(_x)
        else:
            E.wait(ev)

    lda = {}
    g_x16 = {}
    g_q8 = {}
    g_x1 = {}
    g_y16 = {}
    g_yT = {}
    g_v = {}

    def plan_lda(t):
        if t >= NT_A:
            return
        bi = t % 2
        LD.wait(ev_xin_free[bi])
        lda[t] = [LD.op(lambda e, t=t, u=u, bi=bi: e.dma_start(
            out=xin[bi][:, u * 1024:(u + 1) * 1024],
            in_=x_h[t * 512 + u * 128: t * 512 + (u + 1) * 128, :]), dma=True)
            for u in range(4)]

    def A1(sN):
        t, u = sN // 4, sN % 4
        bi = t % 2
        sp = sN % 2
        xa = xin[bi][:, u * 1024:(u + 1) * 1024]
        DVE.wait(ev_x16a_free2[sp])
        ACT.wait(ev_x16a_free2[sp])
        ev = plan_ln(xa, x16a[sp], bnbA[sp], statsA[sp],
                     lda[t][u], ev_stA_free[sp], ev_x16a_free[sp])
        ev_stA_free[sp] = ev
        g_x16[sN] = ev

    def A2(sN):
        sp = sN % 2
        PE.wait(g_x16[sN])
        PE.wait(ev_misc_free2[0])
        PE.wait(ev_misc_free3[0])
        if not ev_consts_waited[0]:
            PE.wait(ev_consts)
            ev_consts_waited[0] = True
        ev_tr = None
        for c in range(DC):
            ev_tr = PE.op(lambda e, c=c, sp=sp: e.transpose(
                miscp16[:, c * 128:(c + 1) * 128], x16a[sp][:, c * 128:(c + 1) * 128], ident[:]))
        ACT.wait(ev_tr)
        ACT.wait(ev_q8T_free[sp])
        q8dst = q8T.ap().rearrange("p (c t) -> p c t", c=DC)[:, :, sp * 128:(sp + 1) * 128]
        ev_q8 = ACT.op(lambda e, q8dst=q8dst: e.activation(
            out=q8dst, in_=miscp16[:, 0:1024], func=AF.Identity, scale=sinv_c[:]))
        ev_misc_free2[0] = ev_q8
        ev_misc_free3[0] = ev_q8
        ev_x16a_free[sp] = ev_q8
        ev_x16a_free2[sp] = ev_q8
        g_q8[sN] = ev_q8

    def A3(sN):
        t, u = sN // 4, sN % 4
        bi = t % 2
        sp = sN % 2
        PE.wait(g_q8[sN])
        PE.wait(ev_ctxpA_free[0])
        PE.wait(ev_ctxpA_free[1])
        ev_f = None
        for half in range(2):
            for c in range(DC):
                ev_f = PE.op(lambda e, c=c, sp=sp, half=half: e.matmul(
                    ctxp[:, half * 512:(half + 1) * 512],
                    q8T[:, c * 256 + sp * 128: c * 256 + (sp + 1) * 128],
                    w8_sb[:, c * 1024 + half * 512: c * 1024 + (half + 1) * 512],
                    start=(c == 0), stop=(c == DC - 1)))
        ev_q8T_free[sp] = ev_f
        DVE.wait(ev_f)
        ev_fa = DVE.op(lambda e: e.tensor_scalar(
            out=f16a[:], in0=ctxp[:, 0:1024], scalar1=pss_c[:], scalar2=None, op0=ALU.mult))
        ev_ctxpA_free[0] = ev_fa
        ev_ctxpA_free[1] = ev_fa
        DVE.op(lambda e: e.tensor_tensor(out=f16a[:], in0=f16a[:], in1=fb_bc[:], op=ALU.add))
        if u == 0:
            wait_all(DVE, ev_x1s_free[bi])
        xa = xin[bi][:, u * 1024:(u + 1) * 1024]
        x1a = x1s[bi][:, u * 1024:(u + 1) * 1024]
        g_x1[sN] = DVE.op(lambda e, xa=xa, x1a=x1a: e.scalar_tensor_tensor(
            out=x1a, in0=f16a[:], scalar=0.5, in1=xa, op0=ALU.mult, op1=ALU.add))

    def A4(sN):
        t, u = sN // 4, sN % 4
        bi = t % 2
        sp = sN % 2
        x1a = x1s[bi][:, u * 1024:(u + 1) * 1024]
        ev = plan_ln(x1a, x16b[sp], bnbB[sp], statsB[sp],
                     g_x1[sN], ev_stB_free[sp], ev_x16b_free[sp])
        ev_stB_free[sp] = ev
        g_y16[sN] = ev

    def A5(sN):
        t, u = sN // 4, sN % 4
        sp = sN % 2
        PE.wait(g_y16[sN])
        PE.wait(ev_misc_free2[0])
        PE.wait(ev_misc_free3[0])
        ev_tr = None
        for c in range(DC):
            ev_tr = PE.op(lambda e, c=c, sp=sp: e.transpose(
                miscp16[:, c * 128:(c + 1) * 128], x16b[sp][:, c * 128:(c + 1) * 128], ident[:]))
        if t > 0 and u == 0:
            DVE.wait(ev_y16T_free[0])
        DVE.wait(ev_yslice_free[u])
        DVE.wait(ev_tr)
        yTdst = y16T.ap().rearrange("p (c t) -> p c t", c=DC)[:, :, u * 128:(u + 1) * 128]
        ev_yT = DVE.op(lambda e, yTdst=yTdst: e.tensor_copy(yTdst, miscp16[:, 0:1024]))
        ev_misc_free2[0] = ev_yT
        ev_misc_free3[0] = ev_yT
        ev_x16b_free[sp] = ev_yT
        g_yT[sN] = ev_yT

    def A6(sN):
        t, u = sN // 4, sN % 4
        PE.wait(g_yT[sN])
        PE.wait(ev_ctxpA_free[2])
        PE.wait(ev_ctxpA_free[3])
        ev_v = None
        for half in range(2):
            for c in range(DC):
                ev_v = PE.op(lambda e, c=c, u=u, half=half: e.matmul(
                    ctxp[:, 1024 + half * 512: 1024 + (half + 1) * 512],
                    y16T[:, c * 512 + u * 128: c * 512 + (u + 1) * 128],
                    wv_sb[:, c * 1024 + half * 512: c * 1024 + (half + 1) * 512],
                    start=(c == 0), stop=(c == DC - 1)))
        ev_yslice_free[u] = ev_v
        DVE.wait(ev_v)
        if t > 0 and u == 0:
            wait_all(DVE, knst["vtm"])
        g_v[sN] = DVE.op(lambda e, u=u: e.scalar_tensor_tensor(
            out=vtm[:, u * 1024:(u + 1) * 1024], in0=ctxp[:, 1024:2048], scalar=1.0,
            in1=bv_bc[:], op0=ALU.mult, op1=ALU.add))
        ev_ctxpA_free[2] = g_v[sN]
        ev_ctxpA_free[3] = g_v[sN]

    def Pr(t):
        PE.wait(g_yT[4 * t + 3])
        DVE.wait(ev_bq32)
        ev_qevac = None
        ev_kevac = None
        ev_kn = None
        ev_mm = None
        ev_sq_l = [None] * DC
        ev_qn_last = None
        for qout in (True, False):
            wsb = wq_sb if qout else wk_sb
            for co in range(DC):
                pbank = stp[co % 2]
                PE.wait(ev_stp_free[co % 2])
                for c in range(DC):
                    ev_mm = PE.op(lambda e, c=c, co=co, wsb=wsb, pbank=pbank: e.matmul(
                        pbank[:], wsb[:, c * 1024 + co * 128: c * 1024 + co * 128 + 128],
                        y16T[:, c * 512:(c + 1) * 512],
                        start=(c == 0), stop=(c == DC - 1)))
                ACT.wait(ev_mm)
                if qout:
                    ACT.wait(knst["qdm"] if co == 0 else None)
                    ev_qevac = ACT.op(lambda e, co=co, pbank=pbank: e.activation(
                        out=qdm[:, co * 512:(co + 1) * 512], in_=pbank[:],
                        func=AF.Identity, scale=1.0 / 32.0, bias=bqd[:, co:co + 1]))
                    ev_stp_free[co % 2] = ev_qevac
                    src = qdm
                else:
                    ACT.wait(knst["kdm"] if co == 0 else None)
                    if co == 0:
                        ACT.wait(ev_qn_last)
                    ev_kevac = ACT.op(lambda e, co=co, pbank=pbank: e.activation(
                        out=kdm[:, co * 512:(co + 1) * 512], in_=pbank[:],
                        func=AF.Identity, bias=bkd[:, co:co + 1]))
                    ev_stp_free[co % 2] = ev_kevac
                    src = kdm
                ev_sq_l[co] = ACT.op(lambda e, co=co, src=src: e.activation(
                    out=ksq[co % 2][:], in_=src[:, co * 512:(co + 1) * 512],
                    func=AF.Square))
                if co >= 1:
                    PE.wait(ev_sq_l[co - 1])
                    if qout and co == 1:
                        PE.wait(knst["prev"])
                    if not qout and co == 1:
                        PE.wait(knst["lnq"])
                    ev_kn = PE.op(lambda e, co=co: e.matmul(
                        sump[0:1, 0:512], ones_col16[:], ksq[(co - 1) % 2][:],
                        start=(co == 1), stop=False))
                if co == DC - 1:
                    PE.wait(ev_sq_l[co])
                    ev_kn = PE.op(lambda e, co=co: e.matmul(
                        sump[0:1, 0:512], ones_col16[:], ksq[co % 2][:],
                        start=False, stop=True))
            if qout:
                ev_qn_last = ev_kn
                ACT.wait(ev_kn)
                ev_lnq = ACT.op(lambda e: e.activation(out=lnqA[:], in_=sump[0:1, 0:512],
                                                       func=AF.Ln))
                knst["lnq"] = ev_lnq
                DVE.wait(ev_lnq)
                for h in range(2):
                    knst["mall"][2 * t + h] = DVE.op(lambda e, t=t, h=h: e.reduce_max(
                        mall[0:1, 2 * t + h: 2 * t + h + 1],
                        lnqA[0:1, h * 256:(h + 1) * 256], axis=AX.X))
                STQ.wait(ev_qevac)
                ev_stq = STQ.op(lambda e, t=t: e.dma_start(
                    out=qd[:, :, t * 512:(t + 1) * 512].rearrange("c p t -> p c t"),
                    in_=qdm[:]), dma=True)
                knst["qdm"] = ev_stq
        DVE.wait(ev_kn)
        DVE.wait(knst["kmax"])
        ev_knmax = DVE.op(lambda e: e.reduce_max(tmp11[:], sump[0:1, 0:512], axis=AX.X))
        knst["prev"] = ev_knmax
        knst["knmax"] = ev_knmax
        knst["kmax"] = DVE.op(lambda e: e.tensor_tensor(out=kmax2[:], in0=kmax2[:],
                                                        in1=tmp11[:], op=ALU.max))
        STQ.wait(ev_kevac)
        ev_stk = STQ.op(lambda e, t=t: e.dma_start(
            out=kd[:, :, t * 512:(t + 1) * 512].rearrange("c p t -> p c t"),
            in_=kdm[:]), dma=True)
        knst["kdm"] = ev_stk
        ev_y16T_free[0] = ev_mm

    plan_lda(0)
    plan_lda(1)
    for g in range(4 * NT_A + 8):
        s6 = g - 8
        if 0 <= s6 < 4 * NT_A:
            A6(s6)
            if s6 % 4 == 3:
                t6 = s6 // 4
                STQ.wait(g_v[s6])
                knst["vtm"] = [STQ.op(lambda e, t6=t6, h=h: e.dma_start(
                    out=vt[t6 * 512 + h * 256: t6 * 512 + (h + 1) * 256, :].rearrange(
                        "(kc p) d -> p kc d", p=128),
                    in_=vtm[:, h * 2048:(h + 1) * 2048]), dma=True) for h in range(2)]
        s5 = g - 6
        if 0 <= s5 < 4 * NT_A:
            A5(s5)
        s4 = g - 4
        if 0 <= s4 < 4 * NT_A:
            A4(s4)
        s3 = g - 3
        if 0 <= s3 < 4 * NT_A:
            A3(s3)
            if s3 % 4 == 3:
                t3 = s3 // 4
                bi3 = t3 % 2
                STQ.wait(g_x1[s3])
                ev_stx1 = [STQ.op(lambda e, t3=t3, bi3=bi3, h=h: e.dma_start(
                    out=x1d[t3 * 512 + h * 256: t3 * 512 + (h + 1) * 256, :].rearrange(
                        "(u p) d -> p u d", p=128),
                    in_=x1s[bi3][:, h * 2048:(h + 1) * 2048]), dma=True) for h in range(2)]
                ev_x1s_free[bi3] = ev_stx1
                ev_xin_free[bi3] = g_x1[s3]
                plan_lda(t3 + 2)
        s2 = g - 2
        if 0 <= s2 < 4 * NT_A:
            A2(s2)
        if g < 4 * NT_A:
            A1(g)
        s5n = g - 5
        if s5n % 4 == 0 and 4 <= s5n <= 4 * NT_A:
            Pr(s5n // 4 - 1)

    ev_qdm_free = knst["qdm"]
    ev_kdm_free = knst["kdm"]
    ev_knmax = knst["knmax"]
    ev_kmax_upd = knst["kmax"]

    ACT.wait(ev_kmax_upd)
    ev_lnk = ACT.op(lambda e: e.activation(out=lnk2h[:], in_=kmax2[:], func=AF.Ln))
    DVE.wait(ev_lnk)
    ev_lnk2 = DVE.op(lambda e: e.tensor_scalar(out=lnk2h[:], in0=lnk2h[:], scalar1=0.5,
                                               scalar2=None, op0=ALU.mult))

    for _i, _sem in enumerate(STQ.dma_sems):
        LD.wait(Ev(_sem, STQ.dma_counts[_i]))
    LD.wait(Ev(STQ.sem, STQ.count))

    ev_qtl_free = [ev_qdm_free, ev_qdm_free]
    ev_kv_free = [ev_xin_free[0], ev_xin_free[1], ev_y16T_free[0], ev_y16T_free[0]]
    ev_x1in_free = [ev_x1s_free[0], ev_x1s_free[1]]
    ev_outs_free = [ev_x1s_free[0], ev_x1s_free[1]]
    ev_stpB_free = list(ev_stp_free)
    ev_ctxp_free = list(ev_ctxpA_free)
    ev_sump_free = [ev_knmax]
    ev_qsqB_free = [ev_kdm_free]
    ev_ctx16_free = [ev_kdm_free, ev_kdm_free]
    ev_ctxT_free = [ev_y16T_free[0], ev_y16T_free[0]]
    ev_pt_free = [None, None, None, None]
    ev_ptt_free = [None]
    ev_x2_free = [knst["vtm"], knst["vtm"]]
    ev_negmcol_free = [None, None]
    ev_negmcol_ready = [None, None]
    ev_negm_prev = [None]
    ev_ldq_ev = [None, None]
    ev_ldx1_ev = [None, None]
    last_store = None

    def plan_loads(qi):
        if qi >= NQT:
            return
        bq_ = qi % 2
        LD.wait(ev_qtl_free[bq_])
        evq = [LD.op(lambda e, qi=qi, bq_=bq_, h=h: e.dma_start(
            out=qtl[bq_][:, h * 512:(h + 1) * 512],
            in_=qd[2 * h:2 * (h + 1), :, qi * TQ:(qi + 1) * TQ].rearrange("c p t -> p c t")),
            dma=True) for h in range(4)]
        ev_ldq_ev[bq_] = evq
        wait_all(LD, ev_x1in_free[bq_])
        evx = [LD.op(lambda e, qi=qi, bq_=bq_, h=h: e.dma_start(
            out=x1in[bq_][:, h * 1024:(h + 1) * 1024],
            in_=x1d[qi * TQ + h * 128: qi * TQ + (h + 1) * 128, :]), dma=True)
            for h in range(2)]
        ev_ldx1_ev[bq_] = evx

    def plan_prep(qi):
        bq_ = qi % 2
        for _ev in ev_ldq_ev[bq_]:
            PE.wait(_ev)
        ACT.wait(knst["mall"][qi])
        ACT.wait(ev_lnk2)
        ACT.wait(ev_negm_prev[0])
        ev_em = ACT.op(lambda e, qi=qi: e.activation(
            out=tmpM[:], in_=mall[0:1, qi:qi + 1], func=AF.Exp, scale=0.5, bias=lnk2h[:]))
        DVE.wait(ev_em)
        ev_negm = DVE.op(lambda e: e.tensor_scalar(
            out=negm[:], in0=tmpM[:], scalar1=-1.0, scalar2=None, op0=ALU.mult))
        ev_negm_prev[0] = ev_negm
        PE.wait(ev_negm)
        PE.wait(ev_misc_free2[0])
        PE.wait(ev_misc_free3[0])
        ev_bc = PE.op(lambda e: e.matmul(miscp[:, 0:1], ones_row32[:], negm[:], start=True, stop=True))
        ev_misc_free2[0] = ev_bc
        DVE.wait(ev_bc)
        DVE.wait(ev_negmcol_free[bq_])
        ev_negmcol_ready[bq_] = DVE.op(lambda e, bq_=bq_: e.tensor_copy(negmcol[bq_][:], miscp[:, 0:1]))

    slab_j = [None] * 4
    slab_ldk = [None] * 4
    slab_ldv = [None] * 4

    def ensure_loaded(j):
        bj = j % 4
        if slab_j[bj] == j:
            return
        LD.wait(ev_kv_free[bj])
        slab_ldk[bj] = [
            LD.op(lambda e, j=j, bj=bj, h=h: e.dma_start(
                out=kst[bj][:, h * 2048:(h + 1) * 2048],
                in_=kd[4 * h:4 * (h + 1), :, j * SUP:(j + 1) * SUP].rearrange("c p t -> p c t")),
                dma=True)
            for h in range(2)]
        slab_ldv[bj] = [
            LD.op(lambda e, j=j, bj=bj, h=h: e.dma_start(
                out=vst[bj][:, h * 2048:(h + 1) * 2048],
                in_=vt[j * SUP + h * 256: j * SUP + (h + 1) * 256, :].rearrange(
                    "(kc p) d -> p kc d", p=128)),
                dma=True)
            for h in range(2)]
        slab_j[bj] = j

    plan_loads(0)
    plan_prep(0)
    carryB = {}

    for qi in range(NQT):
        bq_ = qi % 2
        asc = (qi % 2 == 0)
        nch = (qi + 1) * 2
        nsup = (nch * 128 + SUP - 1) // SUP
        seq = list(range(nch)) if asc else list(range(nch - 1, -1, -1))
        sup_order = list(range(nsup)) if asc else list(range(nsup - 1, -1, -1))
        first_kg = seq[0]
        last_kg = seq[-1]

        state = {"ctx_last": None, "sum_last": None, "ready": None}
        ev_s_of = [None] * nch
        prep_done = [qi == NQT - 1]
        ev_x1bo = [None, None]

        def plan_S(kg, bq_=bq_, ev_s_of=ev_s_of):
            bj = (kg // 4) % 4
            kc = kg % 4
            sp = kg % 2
            pbank = stp[sp]
            for _ev in slab_ldk[bj]:
                PE.wait(_ev)
            PE.wait(ev_stpB_free[sp])
            ev = None
            for c in range(DC):
                ev = PE.op(lambda e, c=c, kc=kc, bj=bj, bq_=bq_, pbank=pbank: e.matmul(
                    pbank[:, 0:TQ],
                    kst[bj][:, c * 512 + kc * 128: c * 512 + (kc + 1) * 128],
                    qtl[bq_][:, c * TQ:(c + 1) * TQ],
                    start=(c == 0), stop=(c == DC - 1)))
            ev_s_of[kg] = ev

        def plan_ctx_sum(kg, ev_ready, first, last, state=state):
            bj = (kg // 4) % 4
            kc = kg % 4
            pti = kg % 4
            PE.wait(ev_ready)
            PE.wait(slab_ldv[bj][kc // 2])
            for u in range(2):
                if first:
                    PE.wait(ev_ctxp_free[2 * u])
                    PE.wait(ev_ctxp_free[2 * u + 1])
                for half in range(2):
                    state["ctx_last"] = PE.op(lambda e, kc=kc, u=u, half=half, bj=bj, pti=pti, first=first, last=last: e.matmul(
                        ctxp[:, u * 1024 + half * 512: u * 1024 + (half + 1) * 512],
                        pt[pti][:, u * 128:(u + 1) * 128],
                        vst[bj][:, kc * 1024 + half * 512: kc * 1024 + (half + 1) * 512],
                        start=first, stop=last))
            if first:
                PE.wait(ev_sump_free[0])
            state["sum_last"] = PE.op(lambda e, pti=pti, first=first, last=last: e.matmul(
                sump[0:1, 0:TQ], ones_col16[:], pt[pti][:, 0:TQ],
                start=first, stop=last))
            ev_pt_free[pti] = state["sum_last"]

        def plan_exp(kg, nch=nch, bq_=bq_, ev_s_of=ev_s_of, state=state):
            sp = kg % 2
            pti = kg % 4
            pbank = stp[sp]
            diag = (kg >= nch - 2)
            ACT.wait(ev_s_of[kg])
            ACT.wait(ev_negmcol_ready[bq_])
            if diag:
                ACT.wait(ev_ptt_free[0])
                ev_exp = ACT.op(lambda e, pbank=pbank, bq_=bq_: e.activation(
                    out=ptt[:], in_=pbank[:, 0:TQ], func=AF.Exp, bias=negmcol[bq_][:]))
                DVE.wait(ev_exp)
                DVE.wait(ev_pt_free[pti])
                m = maskA if (kg == nch - 2) else maskB
                ev_ready = DVE.op(lambda e, pti=pti, m=m: e.tensor_tensor(
                    out=pt[pti][:], in0=ptt[:], in1=m[:], op=ALU.mult))
                ev_ptt_free[0] = ev_ready
            else:
                ACT.wait(ev_pt_free[pti])
                ev_ready = ACT.op(lambda e, pti=pti, pbank=pbank, bq_=bq_: e.activation(
                    out=pt[pti][:], in_=pbank[:, 0:TQ], func=AF.Exp, bias=negmcol[bq_][:]))
            state["ready"] = ev_ready
            ev_stpB_free[sp] = ev_ready
            return ev_ready

        ensure_loaded(sup_order[0])
        if not prep_done[0]:
            plan_loads(qi + 1)
        for _a in (1, 2, 3):
            if nsup > _a:
                ensure_loaded(sup_order[_a])
        if first_kg not in carryB:
            plan_S(first_kg)
        for idx, kg in enumerate(seq):
            nxt = seq[idx + 1] if idx + 1 < nch else None
            if nxt is not None and nxt not in carryB:
                plan_S(nxt)
            if kg in carryB:
                ev_s_of[kg], _rdy = carryB.pop(kg)
                plan_ctx_sum(kg, _rdy, idx == 0, idx == nch - 1)
            else:
                _rdy = plan_exp(kg)
                plan_ctx_sum(kg, _rdy, idx == 0, idx == nch - 1)
            j = kg // 4
            sup_done = (kg % 4 == 3 or kg == nch - 1) if asc else (kg % 4 == 0)
            if sup_done:
                ev_kv_free[j % 4] = state["ctx_last"]
                pos = sup_order.index(j)
                for ahead in (2, 3):
                    if pos + ahead < nsup:
                        ensure_loaded(sup_order[pos + ahead])
            if idx == min(3, nch - 1):
                for _ev in ev_ldx1_ev[bq_]:
                    DVE.wait(_ev)
                for u in range(2):
                    ev_x1bo[u] = DVE.op(lambda e, u=u, bq_=bq_: e.tensor_tensor(
                        out=x1in[bq_][:, u * 1024:(u + 1) * 1024],
                        in0=x1in[bq_][:, u * 1024:(u + 1) * 1024], in1=bo_bc[:],
                        op=ALU.add))
                if not prep_done[0]:
                    plan_prep(qi + 1)
                    prep_done[0] = True
        if not prep_done[0]:
            plan_prep(qi + 1)
            prep_done[0] = True
        ev_qtl_free[bq_] = ev_s_of[last_kg]
        ev_negmcol_free[bq_] = state["ready"]
        ev_ctx_last = state["ctx_last"]
        ev_sum_last = state["sum_last"]

        DVE.wait(ev_sum_last)
        ev_rrow = DVE.op(lambda e: e.reciprocal(rrow[:], sump[0:1, 0:TQ]))
        ev_sump_free[0] = ev_rrow
        ACT.wait(ev_ctx_last)
        ACT.wait(ev_ctx16_free[0])
        ev_c16_0 = ACT.op(lambda e: e.activation(out=ctx16[0], in_=ctxp[:, 0:1024], func=AF.Copy))
        DVE.wait(ev_ctx_last)
        DVE.wait(ev_ctx16_free[1])
        ev_c16_1 = DVE.op(lambda e: e.tensor_copy(ctx16[1], ctxp[:, 1024:2048]))
        ev_c16 = [ev_c16_0, ev_c16_1]

        if qi + 1 < NQT:
            bq2 = (qi + 1) % 2
            nch2 = (qi + 2) * 2
            asc2 = not asc
            peel_kgs = [0, 1, 2, 3] if asc2 else [nch2 - 1, nch2 - 2, nch2 - 3, nch2 - 4]
            ensure_loaded(peel_kgs[0] // 4)
            for kg2 in peel_kgs:
                bj2 = (kg2 // 4) % 4
                kc2 = kg2 % 4
                sp2 = kg2 % 2
                pti2 = kg2 % 4
                pbank2 = stp[sp2]
                for _ev in slab_ldk[bj2]:
                    PE.wait(_ev)
                PE.wait(ev_stpB_free[sp2])
                ev_s2 = None
                for c in range(DC):
                    ev_s2 = PE.op(lambda e, c=c, kc2=kc2, bj2=bj2, bq2=bq2, pbank2=pbank2: e.matmul(
                        pbank2[:, 0:TQ],
                        kst[bj2][:, c * 512 + kc2 * 128: c * 512 + (kc2 + 1) * 128],
                        qtl[bq2][:, c * TQ:(c + 1) * TQ],
                        start=(c == 0), stop=(c == DC - 1)))
                diag2 = (kg2 >= nch2 - 2)
                ACT.wait(ev_s2)
                ACT.wait(ev_negmcol_ready[bq2])
                if diag2:
                    ACT.wait(ev_ptt_free[0])
                    ev_e2 = ACT.op(lambda e, pbank2=pbank2, bq2=bq2: e.activation(
                        out=ptt[:], in_=pbank2[:, 0:TQ], func=AF.Exp, bias=negmcol[bq2][:]))
                    DVE.wait(ev_e2)
                    DVE.wait(ev_pt_free[pti2])
                    m2 = maskA if (kg2 == nch2 - 2) else maskB
                    ev_r2 = DVE.op(lambda e, pti2=pti2, m2=m2: e.tensor_tensor(
                        out=pt[pti2][:], in0=ptt[:], in1=m2[:], op=ALU.mult))
                    ev_ptt_free[0] = ev_r2
                else:
                    ACT.wait(ev_pt_free[pti2])
                    ev_r2 = ACT.op(lambda e, pti2=pti2, pbank2=pbank2, bq2=bq2: e.activation(
                        out=pt[pti2][:], in_=pbank2[:, 0:TQ], func=AF.Exp, bias=negmcol[bq2][:]))
                ev_stpB_free[sp2] = ev_r2
                carryB[kg2] = (ev_s2, ev_r2)

        PE.wait(ev_rrow)
        PE.wait(ev_misc_free2[0])
        PE.wait(ev_misc_free3[0])
        PE.wait(ev_negmcol_ready[1 - bq_])
        ev_rc = []
        for u in range(2):
            ev_rc.append(PE.op(lambda e, u=u: e.matmul(
                miscp[:, 1 + u: 2 + u], rrow[:, u * 128:(u + 1) * 128], one11[:],
                start=True, stop=True)))
        DVE.wait(ev_rc[-1])
        ev_rcol = []
        for u in range(2):
            ev_rcol.append(DVE.op(lambda e, u=u: e.tensor_copy(rcol[u][:], miscp[:, 1 + u:2 + u])))
        ev_wo_l = [None, None]
        ev_trc = [None, None]
        ev_cT = [None, None]
        for u in range(2):
            PE.wait(ev_c16[u])
            PE.wait(ev_misc_free2[0])
            PE.wait(ev_misc_free3[0])
            PE.wait(ev_rcol[-1])
            if u == 1:
                PE.wait(ev_cT[0])
            ev_tr = None
            for c in range(DC):
                ev_tr = PE.op(lambda e, u=u, c=c: e.transpose(
                    miscp16[:, c * 128:(c + 1) * 128], ctx16[u][:, c * 128:(c + 1) * 128], ident[:]))
            ev_trc[u] = ev_tr
            DVE.wait(ev_tr)
            DVE.wait(ev_ctxT_free[u])
            evc = DVE.op(lambda e, u=u: e.tensor_copy(ctxT[u], miscp16[:, 0:1024]))
            ev_cT[u] = evc
            ev_misc_free2[0] = evc
            ev_misc_free3[0] = evc
            PE.wait(evc)
            for half in range(2):
                for c in range(DC):
                    ev_wo_l[u] = PE.op(lambda e, u=u, half=half, c=c: e.matmul(
                        ctxp[:, u * 1024 + half * 512: u * 1024 + (half + 1) * 512],
                        ctxT[u][:, c * 128:(c + 1) * 128],
                        wo_sb[:, c * 1024 + half * 512: c * 1024 + (half + 1) * 512],
                        start=(c == 0), stop=(c == DC - 1)))
            ev_ctx16_free[u] = ev_trc[u]
            ev_ctxT_free[u] = ev_wo_l[u]
        ev_x2 = [None, None]
        for u in range(2):
            ACT.wait(ev_wo_l[u])
            ACT.wait(ev_rcol[u])
            wait_all(ACT, ev_x2_free[u])
            ev_ev = ACT.op(lambda e, u=u: e.activation(
                out=x2f[u], in_=ctxp[:, u * 1024:(u + 1) * 1024],
                func=AF.Identity, scale=rcol[u][:]))
            ev_ctxp_free[2 * u] = ev_ev
            ev_ctxp_free[2 * u + 1] = ev_ev
            DVE.wait(ev_ev)
            DVE.wait(ev_x1bo[u])
            ev_x2[u] = DVE.op(lambda e, u=u, bq_=bq_: e.tensor_tensor(
                out=x2f[u], in0=x2f[u], in1=x1in[bq_][:, u * 1024:(u + 1) * 1024], op=ALU.add))
        ev_x1in_free[bq_] = ev_x2[1]

        STQ.wait(ev_x2[0])
        STQ.wait(ev_x2[1])
        ev_stx2 = STQ.op(lambda e, qi=qi: e.dma_start(
            out=x1d[qi * TQ:(qi + 1) * TQ, :].rearrange("(u p) d -> p u d", p=128),
            in_=vtm.ap().bitcast(F32)[:, 0:2048]), dma=True)
        ev_x2_free[0] = ev_stx2
        ev_x2_free[1] = ev_stx2

    for _i, _sem in enumerate(STQ.dma_sems):
        LD.wait(Ev(_sem, STQ.dma_counts[_i]))

    NSUB = NT_A * 4
    ev_xinC_free = [ev_kv_free[0], ev_kv_free[1]]
    def as_list(ev):
        return ev if isinstance(ev, list) else [ev]

    ev_outC_free = [as_list(ev_x1in_free[0]) + as_list(ev_outs_free[0]),
                    as_list(ev_x1in_free[1]) + as_list(ev_outs_free[1])]
    ev_q8TC_free = [Ev(PE.sem, PE.count), Ev(PE.sem, PE.count)]
    ev_ctxpC_free = [[ev_ctxp_free[0], ev_ctxp_free[1]],
                     [ev_ctxp_free[2], ev_ctxp_free[3]]]
    fC = [wv_sb[:, 0:1024], wv_sb[:, 1024:2048]]
    ev_fC_free = [Ev(DVE.sem, DVE.count), Ev(DVE.sem, DVE.count)]
    stpT16 = [stp[i].ap().bitcast(BF16) for i in range(2)]
    ev_stpC_free = [ev_stpB_free[0], ev_stpB_free[1]]
    ldC = {}
    st_l3g = {}
    st_q8g = {}
    st_outg = {}
    last_store = None

    def planC_loads(t):
        if t >= NT_A:
            return
        bi = t % 2
        LD.wait(ev_xinC_free[bi])
        ldC[t] = [LD.op(lambda e, t=t, u=u, bi=bi: e.dma_start(
            out=xin[bi][:, u * 1024:(u + 1) * 1024],
            in_=x1d[t * 512 + u * 128: t * 512 + (u + 1) * 128, :]), dma=True)
            for u in range(4)]

    def C1(sN):
        t, u = sN // 4, sN % 4
        bi = t % 2
        sp = sN % 2
        xa = xin[bi][:, u * 1024:(u + 1) * 1024]
        ev = plan_ln(xa, x16a[sp], bnbA[sp], statsA[sp],
                     ldC[t][u], ev_stA_free[sp], ev_x16a_free[sp], apply_act=True)
        ev_stA_free[sp] = ev
        st_l3g[sN] = ev

    def C2(sN):
        sp = sN % 2
        PE.wait(st_l3g[sN])
        PE.wait(ev_stpC_free[sp])
        ev_tr = None
        for c in range(DC):
            ev_tr = PE.op(lambda e, c=c, sp=sp: e.transpose(
                stpT16[sp][:, c * 128:(c + 1) * 128], x16a[sp][:, c * 128:(c + 1) * 128], ident[:]))
        ACT.wait(ev_tr)
        ACT.wait(ev_q8TC_free[sp])
        q8dst = q8T.ap().rearrange("p (c t) -> p c t", c=DC)[:, :, sp * 128:(sp + 1) * 128]
        ev_q8 = ACT.op(lambda e, q8dst=q8dst, sp=sp: e.activation(
            out=q8dst, in_=stpT16[sp][:, 0:1024], func=AF.Identity, scale=sinv_c[:]))
        ev_stpC_free[sp] = ev_q8
        ev_x16a_free[sp] = ev_q8
        ev_x16a_free2[sp] = ev_q8
        st_q8g[sN] = ev_q8

    def C3(sN):
        t, u = sN // 4, sN % 4
        bi = t % 2
        sp = sN % 2
        pcb = ctxp[:, sp * 1024:(sp + 1) * 1024]
        PE.wait(st_q8g[sN])
        for _ev in ev_ctxpC_free[sp]:
            PE.wait(_ev)
        ev_f = None
        for half in range(2):
            for c in range(DC):
                ev_f = PE.op(lambda e, c=c, sp=sp, half=half, pcb=pcb: e.matmul(
                    pcb[:, half * 512:(half + 1) * 512],
                    q8T[:, c * 256 + sp * 128: c * 256 + (sp + 1) * 128],
                    w8_sb[:, c * 1024 + half * 512: c * 1024 + (half + 1) * 512],
                    start=(c == 0), stop=(c == DC - 1)))
        ev_q8TC_free[sp] = ev_f
        ACT.wait(ev_f)
        ACT.wait(ev_fC_free[sp])
        ev_ev = ACT.op(lambda e, sp=sp, pcb=pcb: e.activation(
            out=fC[sp][:], in_=pcb[:], func=AF.Identity, scale=pss_c[:]))
        ev_ctxpC_free[sp] = [ev_ev]
        DVE.wait(ev_ev)
        DVE.op(lambda e, sp=sp: e.tensor_tensor(out=fC[sp][:], in0=fC[sp][:],
                                                in1=fb_bc[:], op=ALU.add))
        for evx in ev_outC_free[bi]:
            DVE.wait(evx)
        xa = xin[bi][:, u * 1024:(u + 1) * 1024]
        oa = x1s[bi][:, u * 1024:(u + 1) * 1024]
        ev_o = DVE.op(lambda e, sp=sp, xa=xa, oa=oa: e.scalar_tensor_tensor(
            out=oa, in0=fC[sp][:], scalar=0.5, in1=xa, op0=ALU.mult, op1=ALU.add))
        ev_fC_free[sp] = ev_o
        st_outg[sN] = ev_o

    planC_loads(0)
    planC_loads(1)
    for g in range(NSUB + 4):
        if 0 <= g - 4 < NSUB:
            sc = g - 4
            C3(sc)
            if sc % 4 == 3:
                t = sc // 4
                bi = t % 2
                STQ.wait(st_outg[sc])
                ev_sto = STQ.op(lambda e, t=t, bi=bi: e.dma_start(
                    out=out_h[t * 512:(t + 1) * 512, :].rearrange("(u p) d -> p u d", p=128),
                    in_=x1s[bi][:]), dma=True)
                ev_outC_free[bi] = [ev_sto]
                ev_xinC_free[bi] = st_outg[sc]
                last_store = ev_sto
                planC_loads(t + 2)
        if 0 <= g - 2 < NSUB:
            C2(g - 2)
        if g < NSUB:
            C1(g)

    for _i, _sem in enumerate(STQ.dma_sems):
        STQ.raw(lambda e, s_=_sem, v_=STQ.dma_counts[_i]: e.wait_ge(s_, v_))
    s.emit()
    return nc


_NC_CACHE = {}


def kernel(L0_attn_y_all, L0_attn_Wq, L0_attn_bq, L0_attn_Wk, L0_attn_bk,
           L0_attn_Wv, L0_attn_bv, L0_attn_Wo, L0_attn_bo,
           L0_ffn_kernel, L0_ffn_input_scale, L0_ffn_kernel_scale, L0_ffn_bias):
    x = np.asarray(L0_attn_y_all)
    B = x.shape[0]
    assert x.shape == (8, T, D)

    def lay_bias_d(b):
        return np.ascontiguousarray(np.asarray(b, dtype=np.float32).reshape(DC, 128).T)

    common = {
        "L0_attn_Wq": np.ascontiguousarray(np.asarray(L0_attn_Wq, dtype=np.float32)),
        "L0_attn_Wk": np.ascontiguousarray(np.asarray(L0_attn_Wk, dtype=np.float32)),
        "L0_attn_Wv": np.ascontiguousarray(np.asarray(L0_attn_Wv, dtype=np.float32)),
        "L0_attn_Wo": np.ascontiguousarray(np.asarray(L0_attn_Wo, dtype=np.float32)),
        "L0_attn_bq": lay_bias_d(L0_attn_bq),
        "L0_attn_bk": lay_bias_d(L0_attn_bk),
        "L0_attn_bv": np.ascontiguousarray(np.asarray(L0_attn_bv, dtype=np.float32).reshape(1, D)),
        "L0_attn_bo": np.ascontiguousarray(np.asarray(L0_attn_bo, dtype=np.float32).reshape(1, D)),
        "L0_ffn_kernel": np.ascontiguousarray(np.asarray(L0_ffn_kernel).reshape(D, D)),
        "L0_ffn_input_scale": np.ascontiguousarray(np.asarray(L0_ffn_input_scale, dtype=np.float32).reshape(1, 1)),
        "L0_ffn_kernel_scale": np.ascontiguousarray(np.asarray(L0_ffn_kernel_scale, dtype=np.float32).reshape(1, 1)),
        "L0_ffn_bias": np.ascontiguousarray(np.asarray(L0_ffn_bias).reshape(1, D)),
    }
    in_maps = [dict(common, L0_attn_y_all=np.ascontiguousarray(x[b])) for b in range(B)]

    if "nc" not in _NC_CACHE:
        _NC_CACHE["nc"] = build_program()
    nc = _NC_CACHE["nc"]
    res = run_bass_kernel_spmd(nc, in_maps, core_ids=list(range(8)))
    out = np.stack([np.asarray(res.results[b]["out"]) for b in range(B)], axis=0)
    return out.astype(np.float32, copy=False)


# version 20 (final)
# speedup vs baseline: 1.0236x; 1.0236x over previous
import numpy as np
import concourse.bass as bass
import concourse.mybir as mybir
from concourse.bass_utils import run_bass_kernel_spmd

F32 = mybir.dt.float32
BF16 = mybir.dt.bfloat16
FP8 = mybir.dt.float8e4
AF = mybir.ActivationFunctionType
ALU = mybir.AluOpType
AX = mybir.AxisListType

T = 8192
D = 1024
DC = 8
PT_A = 512
NT_A = T // PT_A
TQ = 256
NQT = T // TQ
SUP = 512


class Ev:
    __slots__ = ("sem", "val")

    def __init__(self, sem, val):
        self.sem = sem
        self.val = val


class Eng:
    def __init__(self, nc, block_name, self_sync, n_dma_lanes=0):
        self.block_name = block_name
        self.items = []
        self.sem = nc.alloc_semaphore(f"sm_{block_name}")
        self.count = 0
        self.self_sync = self_sync
        self._waited = {}
        self.dma_sems = [nc.alloc_semaphore(f"dm_{block_name}{i}") for i in range(n_dma_lanes)]
        self.dma_counts = [0] * n_dma_lanes
        self.dma_idx = 0

    def wait(self, ev):
        if ev is None:
            return
        key = ev.sem.num
        if self._waited.get(key, -1) >= ev.val:
            return
        self._waited[key] = ev.val
        self.items.append(("w", ev.sem, ev.val))

    def op(self, fn, dma=False, inc=True):
        if self.self_sync and self.count > 0 and not dma:
            self.wait(Ev(self.sem, self.count))
        if not inc and not dma:
            self.items.append(("n", fn))
            return None
        if dma and self.dma_sems:
            i = self.dma_idx % len(self.dma_sems)
            self.dma_idx += 1
            self.wait(Ev(self.dma_sems[i], self.dma_counts[i]))
            self.items.append(("o", fn, self.dma_sems[i], 16))
            self.dma_counts[i] += 16
            return Ev(self.dma_sems[i], self.dma_counts[i])
        amt = 16 if dma else 1
        self.items.append(("o", fn, self.sem, amt))
        self.count += amt
        return Ev(self.sem, self.count)

    def raw(self, fn):
        self.items.append(("r", fn))

    def _replay(self, raw_eng):
        for it in self.items:
            if it[0] == "w":
                raw_eng.wait_ge(it[1], it[2])
            elif it[0] == "o":
                it[1](raw_eng).then_inc(it[2], it[3])
            elif it[0] == "n":
                it[1](raw_eng)
            else:
                it[1](raw_eng)


class Sched:
    def __init__(self, nc):
        self.nc = nc
        self.engines = []

    def eng(self, block_name, self_sync=False, n_dma_lanes=0):
        e = Eng(self.nc, block_name, self_sync, n_dma_lanes)
        self.engines.append(e)
        return e

    def emit(self):
        with self.nc.Block() as block:
            for e in self.engines:
                getattr(block, e.block_name)(e._replay)


def build_program(debug_scratch=False):
    nc = bass.Bass("TRN2", target_bir_lowering=False)
    scratch_kind = "ExternalOutput" if debug_scratch else "Internal"

    x_h = nc.dram_tensor("L0_attn_y_all", [T, D], F32, kind="ExternalInput").ap()
    wq_h = nc.dram_tensor("L0_attn_Wq", [D, D], F32, kind="ExternalInput").ap()
    wk_h = nc.dram_tensor("L0_attn_Wk", [D, D], F32, kind="ExternalInput").ap()
    wv_h = nc.dram_tensor("L0_attn_Wv", [D, D], F32, kind="ExternalInput").ap()
    wo_h = nc.dram_tensor("L0_attn_Wo", [D, D], F32, kind="ExternalInput").ap()
    bqd_h = nc.dram_tensor("L0_attn_bq", [128, DC], F32, kind="ExternalInput").ap()
    bkd_h = nc.dram_tensor("L0_attn_bk", [128, DC], F32, kind="ExternalInput").ap()
    bvr_h = nc.dram_tensor("L0_attn_bv", [1, D], F32, kind="ExternalInput").ap()
    bor_h = nc.dram_tensor("L0_attn_bo", [1, D], F32, kind="ExternalInput").ap()
    w1_h = nc.dram_tensor("L0_ffn_kernel", [D, D], BF16, kind="ExternalInput").ap()
    sin_h = nc.dram_tensor("L0_ffn_input_scale", [1, 1], F32, kind="ExternalInput").ap()
    skl_h = nc.dram_tensor("L0_ffn_kernel_scale", [1, 1], F32, kind="ExternalInput").ap()
    fb_h = nc.dram_tensor("L0_ffn_bias", [1, D], BF16, kind="ExternalInput").ap()
    out_h = nc.dram_tensor("out", [T, D], F32, kind="ExternalOutput").ap()

    x1d = nc.dram_tensor("x1d", [T, D], F32, kind=scratch_kind).ap()
    qd = nc.dram_tensor("qd", [DC, 128, T], BF16, kind=scratch_kind).ap()
    kd = nc.dram_tensor("kd", [DC, 128, T], BF16, kind=scratch_kind).ap()
    vt = nc.dram_tensor("vt", [T, D], BF16, kind=scratch_kind).ap()

    def sb(name, shape, dt):
        return nc.alloc_sbuf_tensor(name, shape, dt)

    wq_sb = sb("wq_sb", [128, DC * D], BF16)
    wk_sb = sb("wk_sb", [128, DC * D], BF16)
    wv_sb = sb("wv_sb", [128, DC * D], BF16)
    wo_sb = sb("wo_sb", [128, DC * D], BF16)
    w8_sb = sb("w8_sb", [128, DC * D], FP8)
    bqd = sb("bqd", [128, DC], F32)
    bkd = sb("bkd", [128, DC], F32)
    bv_bc = sb("bv_bc", [128, D], F32)
    bo_bc = sb("bo_bc", [128, D], F32)
    fb_bc = sb("fb_bc", [128, D], BF16)
    sin11 = sb("sin11", [1, 1], F32)
    skl11 = sb("skl11", [1, 1], F32)
    sinv11 = sb("sinv11", [1, 1], F32)
    skinv11 = sb("skinv11", [1, 1], F32)
    pss11 = sb("pss11", [1, 1], F32)
    sinv_c = sb("sinv_c", [128, 1], F32)
    skinv_c = sb("skinv_c", [128, 1], F32)
    pss_c = sb("pss_c", [128, 1], F32)
    ident = sb("ident", [128, 128], BF16)
    maskA = sb("maskA", [128, TQ], BF16)
    maskB = sb("maskB", [128, TQ], BF16)
    ones_col16 = sb("ones_col16", [128, 1], BF16)
    ones_row32 = sb("ones_row32", [1, 128], F32)
    ones_row16 = sb("ones_row16", [1, 128], BF16)
    one11 = sb("one11", [1, 1], F32)
    kmax2 = sb("kmax2", [1, 1], F32)
    lnk2h = sb("lnk2h", [1, 1], F32)
    tmp11 = sb("tmp11", [1, 1], F32)
    eps_c = sb("eps_c", [128, 1], F32)

    xin = [sb(f"xin{i}", [128, 4096], F32) for i in range(2)]
    x1s = [sb(f"x1s{i}", [128, 4096], F32) for i in range(2)]
    qdm = sb("qdm", [128, 4096], BF16)
    kdm = sb("kdm", [128, 4096], BF16)
    vtm = sb("vtm", [128, 4096], BF16)
    y16T = sb("y16T", [128, 4096], BF16)
    q8T = sb("q8T", [128, 2048], FP8)
    x16a = [sb(f"x16a_{i}", [128, 1024], BF16) for i in range(2)]
    x16b = [sb(f"x16b_{i}", [128, 1024], BF16) for i in range(2)]
    f16a = sb("f16a", [128, 1024], BF16)
    statsA = [sb(f"statsA{i}", [128, 8], F32) for i in range(2)]
    statsB = [sb(f"statsB{i}", [128, 8], F32) for i in range(2)]
    bnbA = [sb(f"bnbA{i}", [128, 12], F32) for i in range(2)]
    bnbB = [sb(f"bnbB{i}", [128, 12], F32) for i in range(2)]
    ksq = [sb(f"ksq{i}", [128, 512], BF16) for i in range(2)]
    lnqA = sb("lnqA", [1, 512], F32)
    mall = sb("mall", [1, 32], F32)
    tmpM = sb("tmpM", [1, 1], F32)
    ptt = sb("ptt", [128, TQ], BF16)
    pt = [sb(f"pt{i}", [128, TQ], BF16) for i in range(4)]
    lnq = sb("lnq", [1, TQ], F32)
    negm = sb("negm", [1, 1], F32)
    negmcol = [sb(f"negmcol{i}", [128, 1], F32) for i in range(2)]
    rrow = sb("rrow", [1, TQ], F32)
    rcol = [sb(f"rcol{i}", [128, 1], F32) for i in range(2)]

    kst = [xin[0].ap().bitcast(BF16)[:, 0:4096], xin[1].ap().bitcast(BF16)[:, 0:4096],
           wq_sb[:, 0:4096], wk_sb[:, 0:4096]]
    vst = [xin[0].ap().bitcast(BF16)[:, 4096:8192], xin[1].ap().bitcast(BF16)[:, 4096:8192],
           wq_sb[:, 4096:8192], wk_sb[:, 4096:8192]]
    qtl = [qdm[:, i * 2048:(i + 1) * 2048] for i in range(2)]
    qsqB = kdm[:, 0:2048]
    ctx16 = [kdm[:, 2048 + u * 1024: 2048 + (u + 1) * 1024] for u in range(2)]
    ctxT = [y16T[:, u * 1024:(u + 1) * 1024] for u in range(2)]
    x2f = [vtm.ap().bitcast(F32)[:, u * 1024:(u + 1) * 1024] for u in range(2)]
    x1in = [x1s[i][:, 0:2048] for i in range(2)]
    outs = [x1s[i][:, 2048:4096] for i in range(2)]
    bvr = x1s[0][0:1, 0:1024]
    bor = x1s[0][0:1, 1024:2048]
    fbr = x1s[0].ap().bitcast(BF16)[0:1, 4096:5120]

    ctxp = nc.alloc_psum_tensor("ctxp", [128, 2048], F32)
    stp = [nc.alloc_psum_tensor(f"stp{i}", [128, 512], F32) for i in range(2)]
    sump = nc.alloc_psum_tensor("sump", [1, 512], F32)
    miscp = nc.alloc_psum_tensor("miscp", [128, 512], F32)
    miscp16 = miscp.ap().bitcast(BF16)

    s = Sched(nc)
    LD = s.eng("sync", n_dma_lanes=12)
    PE = s.eng("tensor")
    ACT = s.eng("scalar", self_sync=True)
    DVE = s.eng("vector", self_sync=True)
    STQ = s.eng("gpsimd", n_dma_lanes=8)

    ev_smalls = []
    for tgt, srch in [(bqd, bqd_h), (bkd, bkd_h), (sin11, sin_h), (skl11, skl_h)]:
        ev_smalls.append(LD.op(lambda e, t=tgt, h=srch: e.dma_start(out=t[:], in_=h[:]), dma=True))

    ev_ms = []
    ev_ms.append(STQ.op(lambda e: e.memset(ident[:], 0.0)))
    ev_ms.append(STQ.op(lambda e: e.memset(maskA[:], 1.0)))
    ev_ms.append(STQ.op(lambda e: e.memset(maskB[:], 1.0)))
    STQ.wait(ev_ms[2])
    STQ.op(lambda e: e.affine_select(out=ident[:], in_=ident[:],
                                     compare_op=ALU.not_equal, fill=1.0,
                                     base=0, pattern=[[-1, 128]], channel_multiplier=1))
    STQ.op(lambda e: e.affine_select(out=maskA[:], in_=maskA[:],
                                     compare_op=ALU.is_ge, fill=0.0,
                                     base=0, pattern=[[1, TQ]], channel_multiplier=-1))
    ev_consts = STQ.op(lambda e: e.affine_select(out=maskB[:], in_=maskB[:],
                                                 compare_op=ALU.is_ge, fill=0.0,
                                                 base=-128, pattern=[[1, TQ]], channel_multiplier=-1))
    ev_ones = None
    for t, v in [(ones_col16, 1.0), (ones_row32, 1.0), (ones_row16, 1.0),
                 (one11, 1.0), (kmax2, 0.0), (eps_c, 1e-5)]:
        ev_ones = DVE.op(lambda e, t=t, v=v: e.memset(t[:], v))

    stage_free = {("xin", 0): None, ("xin", 1): None, ("x1s", 0): None, ("x1s", 1): None}

    def plan_weight(w_h, w_sb, slab_name, slab):
        ev_last = None
        for half in range(2):
            st = slab[half]
            LD.wait(stage_free[(slab_name, half)])
            ev_ld = LD.op(lambda e, h=half, st=st, w_h=w_h: e.dma_start(
                out=st[:], in_=w_h.rearrange("(c p) d -> p c d", p=128)[:, 4 * h:4 * (h + 1), :]),
                dma=True)
            DVE.wait(ev_ld)
            ev_last = DVE.op(lambda e, h=half, st=st, w_sb=w_sb: e.tensor_copy(
                w_sb[:, h * 4096:(h + 1) * 4096], st[:]))
            stage_free[(slab_name, half)] = ev_last
        return ev_last

    plan_weight(wq_h, wq_sb, "xin", xin)
    ev_wk = plan_weight(wk_h, wk_sb, "xin", xin)
    plan_weight(wv_h, wv_sb, "x1s", x1s)
    ev_wo = plan_weight(wo_h, wo_sb, "x1s", x1s)

    for _e in ev_smalls:
        DVE.wait(_e)
    DVE.op(lambda e: e.reciprocal(sinv11[:], sin11[:]))
    DVE.op(lambda e: e.reciprocal(skinv11[:], skl11[:]))
    ev_pss = DVE.op(lambda e: e.tensor_tensor(out=pss11[:], in0=sin11[:], in1=skl11[:], op=ALU.mult))
    PE.wait(ev_pss)
    PE.wait(ev_ones)
    PE.op(lambda e: e.matmul(miscp[:, 0:1], ones_row32[:], sinv11[:], start=True, stop=True))
    PE.op(lambda e: e.matmul(miscp[:, 1:2], ones_row32[:], skinv11[:], start=True, stop=True))
    bc2 = PE.op(lambda e: e.matmul(miscp[:, 2:3], ones_row32[:], pss11[:], start=True, stop=True))
    DVE.wait(bc2)
    ev_sc = None
    for tgt, col in [(sinv_c, 0), (skinv_c, 1), (pss_c, 2)]:
        ev_sc = DVE.op(lambda e, t=tgt, c=col: e.tensor_copy(t[:], miscp[:, c:c + 1]))

    LD.wait(ev_wo)
    ev_r1 = LD.op(lambda e: e.dma_start(out=bvr, in_=bvr_h[:]), dma=True)
    ev_r2 = LD.op(lambda e: e.dma_start(out=bor, in_=bor_h[:]), dma=True)
    ev_r3 = LD.op(lambda e: e.dma_start(out=fbr, in_=fb_h[:]), dma=True)
    ev_bias_last = None
    PE.wait(ev_r1)
    PE.wait(ev_r2)
    PE.wait(ev_r3)
    PE.wait(ev_sc)
    for (row, bc, o16) in [(bvr, bv_bc, False), (bor, bo_bc, False), (fbr, fb_bc, True)]:
        for half in range(2):
            lhs = ones_row16 if o16 else ones_row32
            pv = PE.op(lambda e, row=row, h=half, lhs=lhs: e.matmul(
                miscp[:, 0:512], lhs[:], row[:, h * 512:(h + 1) * 512], start=True, stop=True))
            if o16:
                ACT.wait(pv)
                ev = ACT.op(lambda e, bc=bc, h=half: e.activation(
                    out=bc[:, h * 512:(h + 1) * 512], in_=miscp[:, 0:512], func=AF.Copy))
            else:
                DVE.wait(pv)
                ev = DVE.op(lambda e, bc=bc, h=half: e.tensor_copy(
                    bc[:, h * 512:(h + 1) * 512], miscp[:, 0:512]))
            PE.wait(ev)
            ev_bias_last = ev
    ev_x1s0_free = ev_bias_last

    ev_bq32 = DVE.op(lambda e: e.tensor_scalar(out=bqd[:], in0=bqd[:], scalar1=1.0 / 32.0,
                                               scalar2=None, op0=ALU.mult))

    ev_w8 = None
    for half in range(2):
        stb = xin[0].ap().bitcast(BF16)[:, 0:4096]
        LD.wait(ev_wk)
        LD.wait(ev_w8)
        ev_ld = LD.op(lambda e, h=half, stb=stb: e.dma_start(
            out=stb, in_=w1_h.rearrange("(c p) d -> p c d", p=128)[:, 4 * h:4 * (h + 1), :]),
            dma=True)
        ACT.wait(ev_ld)
        ACT.wait(ev_sc)
        ev_w8 = ACT.op(lambda e, h=half, stb=stb: e.activation(
            out=w8_sb[:, h * 4096:(h + 1) * 4096], in_=stb, func=AF.Identity,
            scale=skinv_c[:]))

    def plan_ln(src, dst16, bnb, stt, ev_src, ev_stats_free, ev_dst_free):
        DVE.wait(ev_src)
        DVE.wait(ev_stats_free)
        DVE.op(lambda e, src=src, bnb=bnb: e.bn_stats(bnb[:, 0:6], src[:, 0:512]))
        DVE.op(lambda e, src=src, bnb=bnb: e.bn_stats(bnb[:, 6:12], src[:, 512:1024]))
        ev_aggr = DVE.op(lambda e, bnb=bnb, stt=stt: e.bn_aggr(stt[:, 0:2], bnb[:]))
        ACT.wait(ev_aggr)
        ACT.op(lambda e, stt=stt: e.activation(out=stt[:, 2:3], in_=stt[:, 1:2],
                                               func=AF.Ln, bias=eps_c[:]))
        ev_rstd = ACT.op(lambda e, stt=stt: e.activation(out=stt[:, 3:4], in_=stt[:, 2:3],
                                                         func=AF.Exp, scale=-0.5))
        DVE.wait(ev_rstd)
        DVE.op(lambda e, stt=stt: e.tensor_scalar(out=stt[:, 4:5], in0=stt[:, 0:1],
                                                  scalar1=stt[:, 3:4], scalar2=-1.0,
                                                  op0=ALU.mult, op1=ALU.mult))
        DVE.wait(ev_dst_free)
        ev_dst = DVE.op(lambda e, src=src, dst16=dst16, stt=stt: e.tensor_scalar(
            out=dst16[:], in0=src[:], scalar1=stt[:, 3:4], scalar2=stt[:, 4:5],
            op0=ALU.mult, op1=ALU.add))
        return ev_dst

    ev_xin_free = [ev_w8, ev_wk]
    ev_x1s_free = [ev_x1s0_free, ev_wo]
    ev_qdm_free = None
    ev_kdm_free = None
    ev_x16a_free = [None, None]
    ev_x16b_free = [None, None]
    ev_stA_free = [None, None]
    ev_stB_free = [None, None]
    ev_misc_free2 = [ev_bias_last]
    ev_misc_free3 = [ev_bias_last]
    ev_x16a_free2 = [None, None]
    ev_stp_free = [None, None]
    ev_ctxpA_free = [None, None, None, None]
    ev_y16T_free = [None]
    ev_q8T_free = [None, None]
    ev_consts_waited = [False]
    ev_yslice_free = [None, None, None, None]
    knst = {"prev": None, "kmax": ev_ones, "qdm": None, "kdm": None,
            "vtm": None, "knmax": None, "lnq": None, "mall": [None] * 32}

    def wait_all(E, ev):
        if isinstance(ev, list):
            for _x in ev:
                E.wait(_x)
        else:
            E.wait(ev)

    lda = {}
    g_x16 = {}
    g_q8 = {}
    g_x1 = {}
    g_y16 = {}
    g_yT = {}
    g_v = {}

    def plan_lda(t):
        if t >= NT_A:
            return
        bi = t % 2
        LD.wait(ev_xin_free[bi])
        lda[t] = [LD.op(lambda e, t=t, u=u, bi=bi: e.dma_start(
            out=xin[bi][:, u * 1024:(u + 1) * 1024],
            in_=x_h[t * 512 + u * 128: t * 512 + (u + 1) * 128, :]), dma=True)
            for u in range(4)]

    def A1(sN):
        t, u = sN // 4, sN % 4
        bi = t % 2
        sp = sN % 2
        xa = xin[bi][:, u * 1024:(u + 1) * 1024]
        DVE.wait(ev_x16a_free2[sp])
        ACT.wait(ev_x16a_free2[sp])
        ev = plan_ln(xa, x16a[sp], bnbA[sp], statsA[sp],
                     lda[t][u], ev_stA_free[sp], ev_x16a_free[sp])
        ev_stA_free[sp] = ev
        g_x16[sN] = ev

    def A2(sN):
        sp = sN % 2
        PE.wait(g_x16[sN])
        PE.wait(ev_misc_free2[0])
        PE.wait(ev_misc_free3[0])
        if not ev_consts_waited[0]:
            PE.wait(ev_consts)
            ev_consts_waited[0] = True
        ev_tr = None
        for c in range(DC):
            ev_tr = PE.op(lambda e, c=c, sp=sp: e.transpose(
                miscp16[:, c * 128:(c + 1) * 128], x16a[sp][:, c * 128:(c + 1) * 128], ident[:]))
        ACT.wait(ev_tr)
        ACT.wait(ev_q8T_free[sp])
        q8dst = q8T.ap().rearrange("p (c t) -> p c t", c=DC)[:, :, sp * 128:(sp + 1) * 128]
        ev_q8 = ACT.op(lambda e, q8dst=q8dst: e.activation(
            out=q8dst, in_=miscp16[:, 0:1024], func=AF.Identity, scale=sinv_c[:]))
        ev_misc_free2[0] = ev_q8
        ev_misc_free3[0] = ev_q8
        ev_x16a_free[sp] = ev_q8
        ev_x16a_free2[sp] = ev_q8
        g_q8[sN] = ev_q8

    def A3(sN):
        t, u = sN // 4, sN % 4
        bi = t % 2
        sp = sN % 2
        PE.wait(g_q8[sN])
        PE.wait(ev_ctxpA_free[0])
        PE.wait(ev_ctxpA_free[1])
        ev_f = None
        for half in range(2):
            for c in range(DC):
                ev_f = PE.op(lambda e, c=c, sp=sp, half=half: e.matmul(
                    ctxp[:, half * 512:(half + 1) * 512],
                    q8T[:, c * 256 + sp * 128: c * 256 + (sp + 1) * 128],
                    w8_sb[:, c * 1024 + half * 512: c * 1024 + (half + 1) * 512],
                    start=(c == 0), stop=(c == DC - 1)))
        ev_q8T_free[sp] = ev_f
        DVE.wait(ev_f)
        ev_fa = DVE.op(lambda e: e.tensor_scalar(
            out=f16a[:], in0=ctxp[:, 0:1024], scalar1=pss_c[:], scalar2=None, op0=ALU.mult))
        ev_ctxpA_free[0] = ev_fa
        ev_ctxpA_free[1] = ev_fa
        DVE.op(lambda e: e.tensor_tensor(out=f16a[:], in0=f16a[:], in1=fb_bc[:], op=ALU.add))
        if u == 0:
            wait_all(DVE, ev_x1s_free[bi])
        xa = xin[bi][:, u * 1024:(u + 1) * 1024]
        x1a = x1s[bi][:, u * 1024:(u + 1) * 1024]
        g_x1[sN] = DVE.op(lambda e, xa=xa, x1a=x1a: e.scalar_tensor_tensor(
            out=x1a, in0=f16a[:], scalar=0.5, in1=xa, op0=ALU.mult, op1=ALU.add))

    def A4(sN):
        t, u = sN // 4, sN % 4
        bi = t % 2
        sp = sN % 2
        x1a = x1s[bi][:, u * 1024:(u + 1) * 1024]
        ev = plan_ln(x1a, x16b[sp], bnbB[sp], statsB[sp],
                     g_x1[sN], ev_stB_free[sp], ev_x16b_free[sp])
        ev_stB_free[sp] = ev
        g_y16[sN] = ev

    def A5(sN):
        t, u = sN // 4, sN % 4
        sp = sN % 2
        PE.wait(g_y16[sN])
        PE.wait(ev_misc_free2[0])
        PE.wait(ev_misc_free3[0])
        ev_tr = None
        for c in range(DC):
            ev_tr = PE.op(lambda e, c=c, sp=sp: e.transpose(
                miscp16[:, c * 128:(c + 1) * 128], x16b[sp][:, c * 128:(c + 1) * 128], ident[:]))
        if t > 0 and u == 0:
            DVE.wait(ev_y16T_free[0])
        DVE.wait(ev_yslice_free[u])
        DVE.wait(ev_tr)
        yTdst = y16T.ap().rearrange("p (c t) -> p c t", c=DC)[:, :, u * 128:(u + 1) * 128]
        ev_yT = DVE.op(lambda e, yTdst=yTdst: e.tensor_copy(yTdst, miscp16[:, 0:1024]))
        ev_misc_free2[0] = ev_yT
        ev_misc_free3[0] = ev_yT
        ev_x16b_free[sp] = ev_yT
        g_yT[sN] = ev_yT

    def A6(sN):
        t, u = sN // 4, sN % 4
        PE.wait(g_yT[sN])
        PE.wait(ev_ctxpA_free[2])
        PE.wait(ev_ctxpA_free[3])
        ev_v = None
        for half in range(2):
            for c in range(DC):
                ev_v = PE.op(lambda e, c=c, u=u, half=half: e.matmul(
                    ctxp[:, 1024 + half * 512: 1024 + (half + 1) * 512],
                    y16T[:, c * 512 + u * 128: c * 512 + (u + 1) * 128],
                    wv_sb[:, c * 1024 + half * 512: c * 1024 + (half + 1) * 512],
                    start=(c == 0), stop=(c == DC - 1)))
        ev_yslice_free[u] = ev_v
        DVE.wait(ev_v)
        if t > 0 and u == 0:
            wait_all(DVE, knst["vtm"])
        g_v[sN] = DVE.op(lambda e, u=u: e.scalar_tensor_tensor(
            out=vtm[:, u * 1024:(u + 1) * 1024], in0=ctxp[:, 1024:2048], scalar=1.0,
            in1=bv_bc[:], op0=ALU.mult, op1=ALU.add))
        ev_ctxpA_free[2] = g_v[sN]
        ev_ctxpA_free[3] = g_v[sN]

    def Pr(t):
        PE.wait(g_yT[4 * t + 3])
        DVE.wait(ev_bq32)
        ev_qevac = None
        ev_kevac = None
        ev_kn = None
        ev_mm = None
        ev_sq_l = [None] * DC
        ev_kn_l = [None] * DC
        ev_qn_last = None
        for qout in (True, False):
            wsb = wq_sb if qout else wk_sb
            for co in range(DC):
                pbank = stp[co % 2]
                PE.wait(ev_stp_free[co % 2])
                for c in range(DC):
                    ev_mm = PE.op(lambda e, c=c, co=co, wsb=wsb, pbank=pbank: e.matmul(
                        pbank[:], wsb[:, c * 1024 + co * 128: c * 1024 + co * 128 + 128],
                        y16T[:, c * 512:(c + 1) * 512],
                        start=(c == 0), stop=(c == DC - 1)))
                ACT.wait(ev_mm)
                if qout:
                    ACT.wait(knst["qdm"] if co == 0 else None)
                    ev_qevac = ACT.op(lambda e, co=co, pbank=pbank: e.activation(
                        out=qdm[:, co * 512:(co + 1) * 512], in_=pbank[:],
                        func=AF.Identity, scale=1.0 / 32.0, bias=bqd[:, co:co + 1]))
                    ev_stp_free[co % 2] = ev_qevac
                    src = qdm
                else:
                    ACT.wait(knst["kdm"] if co == 0 else None)
                    if co == 0:
                        ACT.wait(ev_qn_last)
                    ev_kevac = ACT.op(lambda e, co=co, pbank=pbank: e.activation(
                        out=kdm[:, co * 512:(co + 1) * 512], in_=pbank[:],
                        func=AF.Identity, bias=bkd[:, co:co + 1]))
                    ev_stp_free[co % 2] = ev_kevac
                    src = kdm
                if co >= 2:
                    PE.wait(ev_sq_l[co - 2])
                    if qout and co == 2:
                        PE.wait(knst["prev"])
                    if not qout and co == 2:
                        PE.wait(knst["lnq"])
                    ev_kn = PE.op(lambda e, co=co: e.matmul(
                        sump[0:1, 0:512], ones_col16[:], ksq[(co - 2) % 2][:],
                        start=(co == 2), stop=False))
                    ev_kn_l[co - 2] = ev_kn
                ev_evac = ev_qevac if qout else ev_kevac
                DVE.wait(ev_evac)
                if qout and co == 0:
                    DVE.wait(knst["prev"])
                if co >= 2:
                    DVE.wait(ev_kn_l[co - 2])
                ev_sq_l[co] = DVE.op(lambda e, co=co, src=src: e.tensor_tensor(
                    out=ksq[co % 2][:], in0=src[:, co * 512:(co + 1) * 512],
                    in1=src[:, co * 512:(co + 1) * 512], op=ALU.mult))
                if co == DC - 1:
                    for cc in (co - 1, co):
                        PE.wait(ev_sq_l[cc])
                        ev_kn = PE.op(lambda e, cc=cc, co=co: e.matmul(
                            sump[0:1, 0:512], ones_col16[:], ksq[cc % 2][:],
                            start=False, stop=(cc == co)))
                        ev_kn_l[cc] = ev_kn
            if qout:
                ev_qn_last = ev_kn
                ACT.wait(ev_kn)
                ev_lnq = ACT.op(lambda e: e.activation(out=lnqA[:], in_=sump[0:1, 0:512],
                                                       func=AF.Ln))
                knst["lnq"] = ev_lnq
                DVE.wait(ev_lnq)
                for h in range(2):
                    knst["mall"][2 * t + h] = DVE.op(lambda e, t=t, h=h: e.reduce_max(
                        mall[0:1, 2 * t + h: 2 * t + h + 1],
                        lnqA[0:1, h * 256:(h + 1) * 256], axis=AX.X))
                STQ.wait(ev_qevac)
                ev_stq = STQ.op(lambda e, t=t: e.dma_start(
                    out=qd[:, :, t * 512:(t + 1) * 512].rearrange("c p t -> p c t"),
                    in_=qdm[:]), dma=True)
                knst["qdm"] = ev_stq
        DVE.wait(ev_kn)
        DVE.wait(knst["kmax"])
        ev_knmax = DVE.op(lambda e: e.reduce_max(tmp11[:], sump[0:1, 0:512], axis=AX.X))
        knst["prev"] = ev_knmax
        knst["knmax"] = ev_knmax
        knst["kmax"] = DVE.op(lambda e: e.tensor_tensor(out=kmax2[:], in0=kmax2[:],
                                                        in1=tmp11[:], op=ALU.max))
        STQ.wait(ev_kevac)
        ev_stk = STQ.op(lambda e, t=t: e.dma_start(
            out=kd[:, :, t * 512:(t + 1) * 512].rearrange("c p t -> p c t"),
            in_=kdm[:]), dma=True)
        knst["kdm"] = ev_stk
        ev_y16T_free[0] = ev_mm

    plan_lda(0)
    plan_lda(1)
    for g in range(4 * NT_A + 8):
        s6 = g - 8
        if 0 <= s6 < 4 * NT_A:
            A6(s6)
            if s6 % 4 == 3:
                t6 = s6 // 4
                STQ.wait(g_v[s6])
                knst["vtm"] = [STQ.op(lambda e, t6=t6, h=h: e.dma_start(
                    out=vt[t6 * 512 + h * 256: t6 * 512 + (h + 1) * 256, :].rearrange(
                        "(kc p) d -> p kc d", p=128),
                    in_=vtm[:, h * 2048:(h + 1) * 2048]), dma=True) for h in range(2)]
        s5 = g - 6
        if 0 <= s5 < 4 * NT_A:
            A5(s5)
        s4 = g - 4
        if 0 <= s4 < 4 * NT_A:
            A4(s4)
        s3 = g - 3
        if 0 <= s3 < 4 * NT_A:
            A3(s3)
            if s3 % 4 == 3:
                t3 = s3 // 4
                bi3 = t3 % 2
                STQ.wait(g_x1[s3])
                ev_stx1 = [STQ.op(lambda e, t3=t3, bi3=bi3, h=h: e.dma_start(
                    out=x1d[t3 * 512 + h * 256: t3 * 512 + (h + 1) * 256, :].rearrange(
                        "(u p) d -> p u d", p=128),
                    in_=x1s[bi3][:, h * 2048:(h + 1) * 2048]), dma=True) for h in range(2)]
                ev_x1s_free[bi3] = ev_stx1
                ev_xin_free[bi3] = g_x1[s3]
                plan_lda(t3 + 2)
        s2 = g - 2
        if 0 <= s2 < 4 * NT_A:
            A2(s2)
        if g < 4 * NT_A:
            A1(g)
        s5n = g - 5
        if s5n % 4 == 0 and 4 <= s5n <= 4 * NT_A:
            Pr(s5n // 4 - 1)

    ev_qdm_free = knst["qdm"]
    ev_kdm_free = knst["kdm"]
    ev_knmax = knst["knmax"]
    ev_kmax_upd = knst["kmax"]

    ACT.wait(ev_kmax_upd)
    ev_lnk = ACT.op(lambda e: e.activation(out=lnk2h[:], in_=kmax2[:], func=AF.Ln))
    DVE.wait(ev_lnk)
    ev_lnk2 = DVE.op(lambda e: e.tensor_scalar(out=lnk2h[:], in0=lnk2h[:], scalar1=0.5,
                                               scalar2=None, op0=ALU.mult))

    for _i, _sem in enumerate(STQ.dma_sems):
        LD.wait(Ev(_sem, STQ.dma_counts[_i]))
    LD.wait(Ev(STQ.sem, STQ.count))

    ev_qtl_free = [ev_qdm_free, ev_qdm_free]
    ev_kv_free = [ev_xin_free[0], ev_xin_free[1], ev_y16T_free[0], ev_y16T_free[0]]
    ev_x1in_free = [ev_x1s_free[0], ev_x1s_free[1]]
    ev_outs_free = [ev_x1s_free[0], ev_x1s_free[1]]
    ev_stpB_free = list(ev_stp_free)
    ev_ctxp_free = list(ev_ctxpA_free)
    ev_sump_free = [ev_knmax]
    ev_qsqB_free = [ev_kdm_free]
    ev_ctx16_free = [ev_kdm_free, ev_kdm_free]
    ev_ctxT_free = [ev_y16T_free[0], ev_y16T_free[0]]
    ev_pt_free = [None, None, None, None]
    ev_ptt_free = [None]
    ev_x2_free = [knst["vtm"], knst["vtm"]]
    ev_negmcol_free = [None, None]
    ev_negmcol_ready = [None, None]
    ev_negm_prev = [None]
    ev_ldq_ev = [None, None]
    ev_ldx1_ev = [None, None]
    last_store = None

    def plan_loads(qi):
        if qi >= NQT:
            return
        bq_ = qi % 2
        LD.wait(ev_qtl_free[bq_])
        evq = [LD.op(lambda e, qi=qi, bq_=bq_, h=h: e.dma_start(
            out=qtl[bq_][:, h * 512:(h + 1) * 512],
            in_=qd[2 * h:2 * (h + 1), :, qi * TQ:(qi + 1) * TQ].rearrange("c p t -> p c t")),
            dma=True) for h in range(4)]
        ev_ldq_ev[bq_] = evq
        wait_all(LD, ev_x1in_free[bq_])
        evx = [LD.op(lambda e, qi=qi, bq_=bq_, h=h: e.dma_start(
            out=x1in[bq_][:, h * 1024:(h + 1) * 1024],
            in_=x1d[qi * TQ + h * 128: qi * TQ + (h + 1) * 128, :]), dma=True)
            for h in range(2)]
        ev_ldx1_ev[bq_] = evx

    def plan_prep(qi):
        bq_ = qi % 2
        for _ev in ev_ldq_ev[bq_]:
            PE.wait(_ev)
        ACT.wait(knst["mall"][qi])
        ACT.wait(ev_lnk2)
        ACT.wait(ev_negm_prev[0])
        ev_em = ACT.op(lambda e, qi=qi: e.activation(
            out=tmpM[:], in_=mall[0:1, qi:qi + 1], func=AF.Exp, scale=0.5, bias=lnk2h[:]))
        DVE.wait(ev_em)
        ev_negm = DVE.op(lambda e: e.tensor_scalar(
            out=negm[:], in0=tmpM[:], scalar1=-1.0, scalar2=None, op0=ALU.mult))
        ev_negm_prev[0] = ev_negm
        PE.wait(ev_negm)
        PE.wait(ev_misc_free2[0])
        PE.wait(ev_misc_free3[0])
        ev_bc = PE.op(lambda e: e.matmul(miscp[:, 0:1], ones_row32[:], negm[:], start=True, stop=True))
        ev_misc_free2[0] = ev_bc
        DVE.wait(ev_bc)
        DVE.wait(ev_negmcol_free[bq_])
        ev_negmcol_ready[bq_] = DVE.op(lambda e, bq_=bq_: e.tensor_copy(negmcol[bq_][:], miscp[:, 0:1]))

    slab_j = [None] * 4
    slab_ldk = [None] * 4
    slab_ldv = [None] * 4

    def ensure_loaded(j):
        bj = j % 4
        if slab_j[bj] == j:
            return
        LD.wait(ev_kv_free[bj])
        slab_ldk[bj] = [
            LD.op(lambda e, j=j, bj=bj, h=h: e.dma_start(
                out=kst[bj][:, h * 2048:(h + 1) * 2048],
                in_=kd[4 * h:4 * (h + 1), :, j * SUP:(j + 1) * SUP].rearrange("c p t -> p c t")),
                dma=True)
            for h in range(2)]
        slab_ldv[bj] = [
            LD.op(lambda e, j=j, bj=bj, h=h: e.dma_start(
                out=vst[bj][:, h * 2048:(h + 1) * 2048],
                in_=vt[j * SUP + h * 256: j * SUP + (h + 1) * 256, :].rearrange(
                    "(kc p) d -> p kc d", p=128)),
                dma=True)
            for h in range(2)]
        slab_j[bj] = j

    plan_loads(0)
    plan_prep(0)
    carryB = {}

    for qi in range(NQT):
        bq_ = qi % 2
        asc = (qi % 2 == 0)
        nch = (qi + 1) * 2
        nsup = (nch * 128 + SUP - 1) // SUP
        seq = list(range(nch)) if asc else list(range(nch - 1, -1, -1))
        sup_order = list(range(nsup)) if asc else list(range(nsup - 1, -1, -1))
        first_kg = seq[0]
        last_kg = seq[-1]

        state = {"ctx_last": None, "sum_last": None, "ready": None}
        ev_s_of = [None] * nch
        prep_done = [qi == NQT - 1]
        ev_x1bo = [None, None]

        def plan_S(kg, bq_=bq_, ev_s_of=ev_s_of):
            bj = (kg // 4) % 4
            kc = kg % 4
            sp = kg % 2
            pbank = stp[sp]
            for _ev in slab_ldk[bj]:
                PE.wait(_ev)
            PE.wait(ev_stpB_free[sp])
            ev = None
            for c in range(DC):
                ev = PE.op(lambda e, c=c, kc=kc, bj=bj, bq_=bq_, pbank=pbank: e.matmul(
                    pbank[:, 0:TQ],
                    kst[bj][:, c * 512 + kc * 128: c * 512 + (kc + 1) * 128],
                    qtl[bq_][:, c * TQ:(c + 1) * TQ],
                    start=(c == 0), stop=(c == DC - 1)))
            ev_s_of[kg] = ev

        def plan_ctx_sum(kg, ev_ready, first, last, state=state):
            bj = (kg // 4) % 4
            kc = kg % 4
            pti = kg % 4
            PE.wait(ev_ready)
            PE.wait(slab_ldv[bj][kc // 2])
            for u in range(2):
                if first:
                    PE.wait(ev_ctxp_free[2 * u])
                    PE.wait(ev_ctxp_free[2 * u + 1])
                for half in range(2):
                    state["ctx_last"] = PE.op(lambda e, kc=kc, u=u, half=half, bj=bj, pti=pti, first=first, last=last: e.matmul(
                        ctxp[:, u * 1024 + half * 512: u * 1024 + (half + 1) * 512],
                        pt[pti][:, u * 128:(u + 1) * 128],
                        vst[bj][:, kc * 1024 + half * 512: kc * 1024 + (half + 1) * 512],
                        start=first, stop=last))
            if first:
                PE.wait(ev_sump_free[0])
            state["sum_last"] = PE.op(lambda e, pti=pti, first=first, last=last: e.matmul(
                sump[0:1, 0:TQ], ones_col16[:], pt[pti][:, 0:TQ],
                start=first, stop=last))
            ev_pt_free[pti] = state["sum_last"]

        def plan_exp(kg, nch=nch, bq_=bq_, ev_s_of=ev_s_of, state=state):
            sp = kg % 2
            pti = kg % 4
            pbank = stp[sp]
            diag = (kg >= nch - 2)
            ACT.wait(ev_s_of[kg])
            ACT.wait(ev_negmcol_ready[bq_])
            if diag:
                ACT.wait(ev_ptt_free[0])
                ev_exp = ACT.op(lambda e, pbank=pbank, bq_=bq_: e.activation(
                    out=ptt[:], in_=pbank[:, 0:TQ], func=AF.Exp, bias=negmcol[bq_][:]))
                DVE.wait(ev_exp)
                DVE.wait(ev_pt_free[pti])
                m = maskA if (kg == nch - 2) else maskB
                ev_ready = DVE.op(lambda e, pti=pti, m=m: e.tensor_tensor(
                    out=pt[pti][:], in0=ptt[:], in1=m[:], op=ALU.mult))
                ev_ptt_free[0] = ev_ready
            else:
                ACT.wait(ev_pt_free[pti])
                ev_ready = ACT.op(lambda e, pti=pti, pbank=pbank, bq_=bq_: e.activation(
                    out=pt[pti][:], in_=pbank[:, 0:TQ], func=AF.Exp, bias=negmcol[bq_][:]))
            state["ready"] = ev_ready
            ev_stpB_free[sp] = ev_ready
            return ev_ready

        ensure_loaded(sup_order[0])
        if not prep_done[0]:
            plan_loads(qi + 1)
        for _a in (1, 2, 3):
            if nsup > _a:
                ensure_loaded(sup_order[_a])
        if first_kg not in carryB:
            plan_S(first_kg)
        for idx, kg in enumerate(seq):
            nxt = seq[idx + 1] if idx + 1 < nch else None
            if nxt is not None and nxt not in carryB:
                plan_S(nxt)
            if kg in carryB:
                ev_s_of[kg], _rdy = carryB.pop(kg)
                plan_ctx_sum(kg, _rdy, idx == 0, idx == nch - 1)
            else:
                _rdy = plan_exp(kg)
                plan_ctx_sum(kg, _rdy, idx == 0, idx == nch - 1)
            j = kg // 4
            sup_done = (kg % 4 == 3 or kg == nch - 1) if asc else (kg % 4 == 0)
            if sup_done:
                ev_kv_free[j % 4] = state["ctx_last"]
                pos = sup_order.index(j)
                for ahead in (2, 3):
                    if pos + ahead < nsup:
                        ensure_loaded(sup_order[pos + ahead])
            if idx == min(3, nch - 1):
                for _ev in ev_ldx1_ev[bq_]:
                    DVE.wait(_ev)
                for u in range(2):
                    ev_x1bo[u] = DVE.op(lambda e, u=u, bq_=bq_: e.tensor_tensor(
                        out=x1in[bq_][:, u * 1024:(u + 1) * 1024],
                        in0=x1in[bq_][:, u * 1024:(u + 1) * 1024], in1=bo_bc[:],
                        op=ALU.add))
                if not prep_done[0]:
                    plan_prep(qi + 1)
                    prep_done[0] = True
        if not prep_done[0]:
            plan_prep(qi + 1)
            prep_done[0] = True
        ev_qtl_free[bq_] = ev_s_of[last_kg]
        ev_negmcol_free[bq_] = state["ready"]
        ev_ctx_last = state["ctx_last"]
        ev_sum_last = state["sum_last"]

        DVE.wait(ev_sum_last)
        ev_rrow = DVE.op(lambda e: e.reciprocal(rrow[:], sump[0:1, 0:TQ]))
        ev_sump_free[0] = ev_rrow
        ACT.wait(ev_ctx_last)
        ACT.wait(ev_ctx16_free[0])
        ev_c16_0 = ACT.op(lambda e: e.activation(out=ctx16[0], in_=ctxp[:, 0:1024], func=AF.Copy))
        DVE.wait(ev_ctx_last)
        DVE.wait(ev_ctx16_free[1])
        ev_c16_1 = DVE.op(lambda e: e.tensor_copy(ctx16[1], ctxp[:, 1024:2048]))
        ev_c16 = [ev_c16_0, ev_c16_1]

        if qi + 1 < NQT:
            bq2 = (qi + 1) % 2
            nch2 = (qi + 2) * 2
            asc2 = not asc
            peel_kgs = [0, 1, 2, 3] if asc2 else [nch2 - 1, nch2 - 2, nch2 - 3, nch2 - 4]
            ensure_loaded(peel_kgs[0] // 4)
            for kg2 in peel_kgs:
                bj2 = (kg2 // 4) % 4
                kc2 = kg2 % 4
                sp2 = kg2 % 2
                pti2 = kg2 % 4
                pbank2 = stp[sp2]
                for _ev in slab_ldk[bj2]:
                    PE.wait(_ev)
                PE.wait(ev_stpB_free[sp2])
                ev_s2 = None
                for c in range(DC):
                    ev_s2 = PE.op(lambda e, c=c, kc2=kc2, bj2=bj2, bq2=bq2, pbank2=pbank2: e.matmul(
                        pbank2[:, 0:TQ],
                        kst[bj2][:, c * 512 + kc2 * 128: c * 512 + (kc2 + 1) * 128],
                        qtl[bq2][:, c * TQ:(c + 1) * TQ],
                        start=(c == 0), stop=(c == DC - 1)))
                diag2 = (kg2 >= nch2 - 2)
                ACT.wait(ev_s2)
                ACT.wait(ev_negmcol_ready[bq2])
                if diag2:
                    ACT.wait(ev_ptt_free[0])
                    ev_e2 = ACT.op(lambda e, pbank2=pbank2, bq2=bq2: e.activation(
                        out=ptt[:], in_=pbank2[:, 0:TQ], func=AF.Exp, bias=negmcol[bq2][:]))
                    DVE.wait(ev_e2)
                    DVE.wait(ev_pt_free[pti2])
                    m2 = maskA if (kg2 == nch2 - 2) else maskB
                    ev_r2 = DVE.op(lambda e, pti2=pti2, m2=m2: e.tensor_tensor(
                        out=pt[pti2][:], in0=ptt[:], in1=m2[:], op=ALU.mult))
                    ev_ptt_free[0] = ev_r2
                else:
                    ACT.wait(ev_pt_free[pti2])
                    ev_r2 = ACT.op(lambda e, pti2=pti2, pbank2=pbank2, bq2=bq2: e.activation(
                        out=pt[pti2][:], in_=pbank2[:, 0:TQ], func=AF.Exp, bias=negmcol[bq2][:]))
                ev_stpB_free[sp2] = ev_r2
                carryB[kg2] = (ev_s2, ev_r2)

        PE.wait(ev_rrow)
        PE.wait(ev_misc_free2[0])
        PE.wait(ev_misc_free3[0])
        PE.wait(ev_negmcol_ready[1 - bq_])
        ev_rc = []
        for u in range(2):
            ev_rc.append(PE.op(lambda e, u=u: e.matmul(
                miscp[:, 1 + u: 2 + u], rrow[:, u * 128:(u + 1) * 128], one11[:],
                start=True, stop=True)))
        DVE.wait(ev_rc[-1])
        ev_rcol = []
        for u in range(2):
            ev_rcol.append(DVE.op(lambda e, u=u: e.tensor_copy(rcol[u][:], miscp[:, 1 + u:2 + u])))
        ev_wo_l = [None, None]
        ev_trc = [None, None]
        ev_cT = [None, None]
        for u in range(2):
            PE.wait(ev_c16[u])
            PE.wait(ev_misc_free2[0])
            PE.wait(ev_misc_free3[0])
            PE.wait(ev_rcol[-1])
            if u == 1:
                PE.wait(ev_cT[0])
            ev_tr = None
            for c in range(DC):
                ev_tr = PE.op(lambda e, u=u, c=c: e.transpose(
                    miscp16[:, c * 128:(c + 1) * 128], ctx16[u][:, c * 128:(c + 1) * 128], ident[:]))
            ev_trc[u] = ev_tr
            DVE.wait(ev_tr)
            DVE.wait(ev_ctxT_free[u])
            evc = DVE.op(lambda e, u=u: e.tensor_copy(ctxT[u], miscp16[:, 0:1024]))
            ev_cT[u] = evc
            ev_misc_free2[0] = evc
            ev_misc_free3[0] = evc
            PE.wait(evc)
            for half in range(2):
                for c in range(DC):
                    ev_wo_l[u] = PE.op(lambda e, u=u, half=half, c=c: e.matmul(
                        ctxp[:, u * 1024 + half * 512: u * 1024 + (half + 1) * 512],
                        ctxT[u][:, c * 128:(c + 1) * 128],
                        wo_sb[:, c * 1024 + half * 512: c * 1024 + (half + 1) * 512],
                        start=(c == 0), stop=(c == DC - 1)))
            ev_ctx16_free[u] = ev_trc[u]
            ev_ctxT_free[u] = ev_wo_l[u]
        ev_x2 = [None, None]
        for u in range(2):
            ACT.wait(ev_wo_l[u])
            ACT.wait(ev_rcol[u])
            wait_all(ACT, ev_x2_free[u])
            ev_ev = ACT.op(lambda e, u=u: e.activation(
                out=x2f[u], in_=ctxp[:, u * 1024:(u + 1) * 1024],
                func=AF.Identity, scale=rcol[u][:]))
            ev_ctxp_free[2 * u] = ev_ev
            ev_ctxp_free[2 * u + 1] = ev_ev
            DVE.wait(ev_ev)
            DVE.wait(ev_x1bo[u])
            ev_x2[u] = DVE.op(lambda e, u=u, bq_=bq_: e.tensor_tensor(
                out=x2f[u], in0=x2f[u], in1=x1in[bq_][:, u * 1024:(u + 1) * 1024], op=ALU.add))
        ev_x1in_free[bq_] = ev_x2[1]

        STQ.wait(ev_x2[0])
        STQ.wait(ev_x2[1])
        ev_stx2 = STQ.op(lambda e, qi=qi: e.dma_start(
            out=x1d[qi * TQ:(qi + 1) * TQ, :].rearrange("(u p) d -> p u d", p=128),
            in_=vtm.ap().bitcast(F32)[:, 0:2048]), dma=True)
        ev_x2_free[0] = ev_stx2
        ev_x2_free[1] = ev_stx2

    for _i, _sem in enumerate(STQ.dma_sems):
        LD.wait(Ev(_sem, STQ.dma_counts[_i]))

    NSUB = NT_A * 4
    ev_xinC_free = [ev_kv_free[0], ev_kv_free[1]]
    def as_list(ev):
        return ev if isinstance(ev, list) else [ev]

    ev_outC_free = [as_list(ev_x1in_free[0]) + as_list(ev_outs_free[0]),
                    as_list(ev_x1in_free[1]) + as_list(ev_outs_free[1])]
    ev_q8TC_free = [Ev(PE.sem, PE.count), Ev(PE.sem, PE.count)]
    ev_ctxpC_free = [[ev_ctxp_free[0], ev_ctxp_free[1]],
                     [ev_ctxp_free[2], ev_ctxp_free[3]]]
    fC = [wv_sb[:, 0:1024], wv_sb[:, 1024:2048]]
    ev_fC_free = [Ev(DVE.sem, DVE.count), Ev(DVE.sem, DVE.count)]
    stpT16 = [stp[i].ap().bitcast(BF16) for i in range(2)]
    ev_stpC_free = [ev_stpB_free[0], ev_stpB_free[1]]
    ldC = {}
    st_l3g = {}
    st_q8g = {}
    st_outg = {}
    last_store = None

    def planC_loads(t):
        if t >= NT_A:
            return
        bi = t % 2
        LD.wait(ev_xinC_free[bi])
        ldC[t] = [LD.op(lambda e, t=t, u=u, bi=bi: e.dma_start(
            out=xin[bi][:, u * 1024:(u + 1) * 1024],
            in_=x1d[t * 512 + u * 128: t * 512 + (u + 1) * 128, :]), dma=True)
            for u in range(4)]

    def C1(sN):
        t, u = sN // 4, sN % 4
        bi = t % 2
        sp = sN % 2
        xa = xin[bi][:, u * 1024:(u + 1) * 1024]
        ev = plan_ln(xa, x16a[sp], bnbA[sp], statsA[sp],
                     ldC[t][u], ev_stA_free[sp], ev_x16a_free[sp])
        ev_stA_free[sp] = ev
        st_l3g[sN] = ev

    def C2(sN):
        sp = sN % 2
        PE.wait(st_l3g[sN])
        PE.wait(ev_stpC_free[sp])
        ev_tr = None
        for c in range(DC):
            ev_tr = PE.op(lambda e, c=c, sp=sp: e.transpose(
                stpT16[sp][:, c * 128:(c + 1) * 128], x16a[sp][:, c * 128:(c + 1) * 128], ident[:]))
        ACT.wait(ev_tr)
        ACT.wait(ev_q8TC_free[sp])
        q8dst = q8T.ap().rearrange("p (c t) -> p c t", c=DC)[:, :, sp * 128:(sp + 1) * 128]
        ev_q8 = ACT.op(lambda e, q8dst=q8dst, sp=sp: e.activation(
            out=q8dst, in_=stpT16[sp][:, 0:1024], func=AF.Identity, scale=sinv_c[:]))
        ev_stpC_free[sp] = ev_q8
        ev_x16a_free[sp] = ev_q8
        ev_x16a_free2[sp] = ev_q8
        st_q8g[sN] = ev_q8

    def C3(sN):
        t, u = sN // 4, sN % 4
        bi = t % 2
        sp = sN % 2
        pcb = ctxp[:, sp * 1024:(sp + 1) * 1024]
        PE.wait(st_q8g[sN])
        for _ev in ev_ctxpC_free[sp]:
            PE.wait(_ev)
        ev_f = None
        for half in range(2):
            for c in range(DC):
                ev_f = PE.op(lambda e, c=c, sp=sp, half=half, pcb=pcb: e.matmul(
                    pcb[:, half * 512:(half + 1) * 512],
                    q8T[:, c * 256 + sp * 128: c * 256 + (sp + 1) * 128],
                    w8_sb[:, c * 1024 + half * 512: c * 1024 + (half + 1) * 512],
                    start=(c == 0), stop=(c == DC - 1)))
        ev_q8TC_free[sp] = ev_f
        ACT.wait(ev_f)
        ACT.wait(ev_fC_free[sp])
        ev_ev = ACT.op(lambda e, sp=sp, pcb=pcb: e.activation(
            out=fC[sp][:], in_=pcb[:], func=AF.Identity, scale=pss_c[:]))
        ev_ctxpC_free[sp] = [ev_ev]
        DVE.wait(ev_ev)
        DVE.op(lambda e, sp=sp: e.tensor_tensor(out=fC[sp][:], in0=fC[sp][:],
                                                in1=fb_bc[:], op=ALU.add))
        for evx in ev_outC_free[bi]:
            DVE.wait(evx)
        xa = xin[bi][:, u * 1024:(u + 1) * 1024]
        oa = x1s[bi][:, u * 1024:(u + 1) * 1024]
        ev_o = DVE.op(lambda e, sp=sp, xa=xa, oa=oa: e.scalar_tensor_tensor(
            out=oa, in0=fC[sp][:], scalar=0.5, in1=xa, op0=ALU.mult, op1=ALU.add))
        ev_fC_free[sp] = ev_o
        st_outg[sN] = ev_o

    planC_loads(0)
    planC_loads(1)
    for g in range(NSUB + 4):
        if 0 <= g - 4 < NSUB:
            sc = g - 4
            C3(sc)
            if sc % 4 == 3:
                t = sc // 4
                bi = t % 2
                STQ.wait(st_outg[sc])
                ev_sto = STQ.op(lambda e, t=t, bi=bi: e.dma_start(
                    out=out_h[t * 512:(t + 1) * 512, :].rearrange("(u p) d -> p u d", p=128),
                    in_=x1s[bi][:]), dma=True)
                ev_outC_free[bi] = [ev_sto]
                ev_xinC_free[bi] = st_outg[sc]
                last_store = ev_sto
                planC_loads(t + 2)
        if 0 <= g - 2 < NSUB:
            C2(g - 2)
        if g < NSUB:
            C1(g)

    for _i, _sem in enumerate(STQ.dma_sems):
        STQ.raw(lambda e, s_=_sem, v_=STQ.dma_counts[_i]: e.wait_ge(s_, v_))
    s.emit()
    return nc


_NC_CACHE = {}


def kernel(L0_attn_y_all, L0_attn_Wq, L0_attn_bq, L0_attn_Wk, L0_attn_bk,
           L0_attn_Wv, L0_attn_bv, L0_attn_Wo, L0_attn_bo,
           L0_ffn_kernel, L0_ffn_input_scale, L0_ffn_kernel_scale, L0_ffn_bias):
    x = np.asarray(L0_attn_y_all)
    B = x.shape[0]
    assert x.shape == (8, T, D)

    def lay_bias_d(b):
        return np.ascontiguousarray(np.asarray(b, dtype=np.float32).reshape(DC, 128).T)

    common = {
        "L0_attn_Wq": np.ascontiguousarray(np.asarray(L0_attn_Wq, dtype=np.float32)),
        "L0_attn_Wk": np.ascontiguousarray(np.asarray(L0_attn_Wk, dtype=np.float32)),
        "L0_attn_Wv": np.ascontiguousarray(np.asarray(L0_attn_Wv, dtype=np.float32)),
        "L0_attn_Wo": np.ascontiguousarray(np.asarray(L0_attn_Wo, dtype=np.float32)),
        "L0_attn_bq": lay_bias_d(L0_attn_bq),
        "L0_attn_bk": lay_bias_d(L0_attn_bk),
        "L0_attn_bv": np.ascontiguousarray(np.asarray(L0_attn_bv, dtype=np.float32).reshape(1, D)),
        "L0_attn_bo": np.ascontiguousarray(np.asarray(L0_attn_bo, dtype=np.float32).reshape(1, D)),
        "L0_ffn_kernel": np.ascontiguousarray(np.asarray(L0_ffn_kernel).reshape(D, D)),
        "L0_ffn_input_scale": np.ascontiguousarray(np.asarray(L0_ffn_input_scale, dtype=np.float32).reshape(1, 1)),
        "L0_ffn_kernel_scale": np.ascontiguousarray(np.asarray(L0_ffn_kernel_scale, dtype=np.float32).reshape(1, 1)),
        "L0_ffn_bias": np.ascontiguousarray(np.asarray(L0_ffn_bias).reshape(1, D)),
    }
    in_maps = [dict(common, L0_attn_y_all=np.ascontiguousarray(x[b])) for b in range(B)]

    if "nc" not in _NC_CACHE:
        _NC_CACHE["nc"] = build_program()
    nc = _NC_CACHE["nc"]
    res = run_bass_kernel_spmd(nc, in_maps, core_ids=list(range(8)))
    out = np.stack([np.asarray(res.results[b]["out"]) for b in range(B)], axis=0)
    return out.astype(np.float32, copy=False)


# version 21
# speedup vs baseline: 1.0300x; 1.0062x over previous
import numpy as np
import concourse.bass as bass
import concourse.mybir as mybir
from concourse.bass_utils import run_bass_kernel_spmd

F32 = mybir.dt.float32
BF16 = mybir.dt.bfloat16
FP8 = mybir.dt.float8e4
AF = mybir.ActivationFunctionType
ALU = mybir.AluOpType
AX = mybir.AxisListType

T = 8192
D = 1024
DC = 8
PT_A = 512
NT_A = T // PT_A
TQ = 256
NQT = T // TQ
SUP = 512


class Ev:
    __slots__ = ("sem", "val")

    def __init__(self, sem, val):
        self.sem = sem
        self.val = val


class Eng:
    def __init__(self, nc, block_name, self_sync, n_dma_lanes=0):
        self.block_name = block_name
        self.items = []
        self.sem = nc.alloc_semaphore(f"sm_{block_name}")
        self.count = 0
        self.self_sync = self_sync
        self._waited = {}
        self.dma_sems = [nc.alloc_semaphore(f"dm_{block_name}{i}") for i in range(n_dma_lanes)]
        self.dma_counts = [0] * n_dma_lanes
        self.dma_idx = 0

    def wait(self, ev):
        if ev is None:
            return
        key = ev.sem.num
        if self._waited.get(key, -1) >= ev.val:
            return
        self._waited[key] = ev.val
        self.items.append(("w", ev.sem, ev.val))

    def op(self, fn, dma=False, inc=True):
        if self.self_sync and self.count > 0 and not dma:
            self.wait(Ev(self.sem, self.count))
        if not inc and not dma:
            self.items.append(("n", fn))
            return None
        if dma and self.dma_sems:
            i = self.dma_idx % len(self.dma_sems)
            self.dma_idx += 1
            self.wait(Ev(self.dma_sems[i], self.dma_counts[i]))
            self.items.append(("o", fn, self.dma_sems[i], 16))
            self.dma_counts[i] += 16
            return Ev(self.dma_sems[i], self.dma_counts[i])
        amt = 16 if dma else 1
        self.items.append(("o", fn, self.sem, amt))
        self.count += amt
        return Ev(self.sem, self.count)

    def raw(self, fn):
        self.items.append(("r", fn))

    def _replay(self, raw_eng):
        for it in self.items:
            if it[0] == "w":
                raw_eng.wait_ge(it[1], it[2])
            elif it[0] == "o":
                it[1](raw_eng).then_inc(it[2], it[3])
            elif it[0] == "n":
                it[1](raw_eng)
            else:
                it[1](raw_eng)


class Sched:
    def __init__(self, nc):
        self.nc = nc
        self.engines = []

    def eng(self, block_name, self_sync=False, n_dma_lanes=0):
        e = Eng(self.nc, block_name, self_sync, n_dma_lanes)
        self.engines.append(e)
        return e

    def emit(self):
        with self.nc.Block() as block:
            for e in self.engines:
                getattr(block, e.block_name)(e._replay)


def build_program(debug_scratch=False):
    nc = bass.Bass("TRN2", target_bir_lowering=False)
    scratch_kind = "ExternalOutput" if debug_scratch else "Internal"

    x_h = nc.dram_tensor("L0_attn_y_all", [T, D], F32, kind="ExternalInput").ap()
    wq_h = nc.dram_tensor("L0_attn_Wq", [D, D], F32, kind="ExternalInput").ap()
    wk_h = nc.dram_tensor("L0_attn_Wk", [D, D], F32, kind="ExternalInput").ap()
    wv_h = nc.dram_tensor("L0_attn_Wv", [D, D], F32, kind="ExternalInput").ap()
    wo_h = nc.dram_tensor("L0_attn_Wo", [D, D], F32, kind="ExternalInput").ap()
    bqd_h = nc.dram_tensor("L0_attn_bq", [128, DC], F32, kind="ExternalInput").ap()
    bkd_h = nc.dram_tensor("L0_attn_bk", [128, DC], F32, kind="ExternalInput").ap()
    bvr_h = nc.dram_tensor("L0_attn_bv", [1, D], F32, kind="ExternalInput").ap()
    bor_h = nc.dram_tensor("L0_attn_bo", [1, D], F32, kind="ExternalInput").ap()
    w1_h = nc.dram_tensor("L0_ffn_kernel", [D, D], BF16, kind="ExternalInput").ap()
    sin_h = nc.dram_tensor("L0_ffn_input_scale", [1, 1], F32, kind="ExternalInput").ap()
    skl_h = nc.dram_tensor("L0_ffn_kernel_scale", [1, 1], F32, kind="ExternalInput").ap()
    fb_h = nc.dram_tensor("L0_ffn_bias", [1, D], BF16, kind="ExternalInput").ap()
    out_h = nc.dram_tensor("out", [T, D], F32, kind="ExternalOutput").ap()

    x1d = nc.dram_tensor("x1d", [T, D], F32, kind=scratch_kind).ap()
    qd = nc.dram_tensor("qd", [DC, 128, T], BF16, kind=scratch_kind).ap()
    kd = nc.dram_tensor("kd", [DC, 128, T], BF16, kind=scratch_kind).ap()
    vt = nc.dram_tensor("vt", [T, D], BF16, kind=scratch_kind).ap()

    def sb(name, shape, dt):
        return nc.alloc_sbuf_tensor(name, shape, dt)

    wq_sb = sb("wq_sb", [128, DC * D], BF16)
    wk_sb = sb("wk_sb", [128, DC * D], BF16)
    wv_sb = sb("wv_sb", [128, DC * D], BF16)
    wo_sb = sb("wo_sb", [128, DC * D], BF16)
    w8_sb = sb("w8_sb", [128, DC * D], FP8)
    bqd = sb("bqd", [128, DC], F32)
    bkd = sb("bkd", [128, DC], F32)
    bv_bc = sb("bv_bc", [128, D], F32)
    bo_bc = sb("bo_bc", [128, D], F32)
    fb_bc = sb("fb_bc", [128, D], BF16)
    sin11 = sb("sin11", [1, 1], F32)
    skl11 = sb("skl11", [1, 1], F32)
    sinv11 = sb("sinv11", [1, 1], F32)
    skinv11 = sb("skinv11", [1, 1], F32)
    pss11 = sb("pss11", [1, 1], F32)
    sinv_c = sb("sinv_c", [128, 1], F32)
    skinv_c = sb("skinv_c", [128, 1], F32)
    pss_c = sb("pss_c", [128, 1], F32)
    ident = sb("ident", [128, 128], BF16)
    maskA = sb("maskA", [128, TQ], BF16)
    maskB = sb("maskB", [128, TQ], BF16)
    ones_col16 = sb("ones_col16", [128, 1], BF16)
    ones_row32 = sb("ones_row32", [1, 128], F32)
    ones_row16 = sb("ones_row16", [1, 128], BF16)
    one11 = sb("one11", [1, 1], F32)
    kmax2 = sb("kmax2", [1, 1], F32)
    lnk2h = sb("lnk2h", [1, 1], F32)
    tmp11 = sb("tmp11", [1, 1], F32)
    eps_c = sb("eps_c", [128, 1], F32)

    xin = [sb(f"xin{i}", [128, 4096], F32) for i in range(2)]
    x1s = [sb(f"x1s{i}", [128, 4096], F32) for i in range(2)]
    qdm = sb("qdm", [128, 4096], BF16)
    kdm = sb("kdm", [128, 4096], BF16)
    vtm = sb("vtm", [128, 4096], BF16)
    y16T = sb("y16T", [128, 4096], BF16)
    q8T = sb("q8T", [128, 2048], FP8)
    x16a = [sb(f"x16a_{i}", [128, 1024], BF16) for i in range(2)]
    x16b = [sb(f"x16b_{i}", [128, 1024], BF16) for i in range(2)]
    f16a = sb("f16a", [128, 1024], BF16)
    statsA = [sb(f"statsA{i}", [128, 8], F32) for i in range(2)]
    statsB = [sb(f"statsB{i}", [128, 8], F32) for i in range(2)]
    bnbA = [sb(f"bnbA{i}", [128, 12], F32) for i in range(2)]
    bnbB = [sb(f"bnbB{i}", [128, 12], F32) for i in range(2)]
    ksq = [sb(f"ksq{i}", [128, 512], BF16) for i in range(2)]
    lnqA = sb("lnqA", [1, 512], F32)
    mall = sb("mall", [1, 32], F32)
    tmpM = sb("tmpM", [1, 1], F32)
    ptt = sb("ptt", [128, TQ], BF16)
    pt = [sb(f"pt{i}", [128, TQ], BF16) for i in range(4)]
    lnq = sb("lnq", [1, TQ], F32)
    negm = sb("negm", [1, 1], F32)
    negmcol = [sb(f"negmcol{i}", [128, 1], F32) for i in range(2)]
    rrow = sb("rrow", [1, TQ], F32)
    rcol = [sb(f"rcol{i}", [128, 1], F32) for i in range(2)]

    kst = [xin[0].ap().bitcast(BF16)[:, 0:4096], xin[1].ap().bitcast(BF16)[:, 0:4096],
           wq_sb[:, 0:4096], wk_sb[:, 0:4096]]
    vst = [xin[0].ap().bitcast(BF16)[:, 4096:8192], xin[1].ap().bitcast(BF16)[:, 4096:8192],
           wq_sb[:, 4096:8192], wk_sb[:, 4096:8192]]
    qtl = [qdm[:, i * 2048:(i + 1) * 2048] for i in range(2)]
    qsqB = kdm[:, 0:2048]
    ctx16 = [kdm[:, 2048 + u * 1024: 2048 + (u + 1) * 1024] for u in range(2)]
    ctxT = [y16T[:, u * 1024:(u + 1) * 1024] for u in range(2)]
    x2f = [vtm.ap().bitcast(F32)[:, u * 1024:(u + 1) * 1024] for u in range(2)]
    x1in = [x1s[i][:, 0:2048] for i in range(2)]
    outs = [x1s[i][:, 2048:4096] for i in range(2)]
    bvr = x1s[0][0:1, 0:1024]
    bor = x1s[0][0:1, 1024:2048]
    fbr = x1s[0].ap().bitcast(BF16)[0:1, 4096:5120]

    ctxp = nc.alloc_psum_tensor("ctxp", [128, 2048], F32)
    stp = [nc.alloc_psum_tensor(f"stp{i}", [128, 512], F32) for i in range(2)]
    sump = nc.alloc_psum_tensor("sump", [1, 512], F32)
    miscp = nc.alloc_psum_tensor("miscp", [128, 512], F32)
    miscp16 = miscp.ap().bitcast(BF16)

    s = Sched(nc)
    LD = s.eng("sync", n_dma_lanes=12)
    PE = s.eng("tensor")
    ACT = s.eng("scalar", self_sync=True)
    DVE = s.eng("vector", self_sync=True)
    STQ = s.eng("gpsimd", n_dma_lanes=8)

    ev_smalls = []
    for tgt, srch in [(bqd, bqd_h), (bkd, bkd_h), (sin11, sin_h), (skl11, skl_h)]:
        ev_smalls.append(LD.op(lambda e, t=tgt, h=srch: e.dma_start(out=t[:], in_=h[:]), dma=True))

    ev_ms = []
    ev_ms.append(STQ.op(lambda e: e.memset(ident[:], 0.0)))
    ev_ms.append(STQ.op(lambda e: e.memset(maskA[:], 1.0)))
    ev_ms.append(STQ.op(lambda e: e.memset(maskB[:], 1.0)))
    STQ.wait(ev_ms[2])
    STQ.op(lambda e: e.affine_select(out=ident[:], in_=ident[:],
                                     compare_op=ALU.not_equal, fill=1.0,
                                     base=0, pattern=[[-1, 128]], channel_multiplier=1))
    STQ.op(lambda e: e.affine_select(out=maskA[:], in_=maskA[:],
                                     compare_op=ALU.is_ge, fill=0.0,
                                     base=0, pattern=[[1, TQ]], channel_multiplier=-1))
    ev_consts = STQ.op(lambda e: e.affine_select(out=maskB[:], in_=maskB[:],
                                                 compare_op=ALU.is_ge, fill=0.0,
                                                 base=-128, pattern=[[1, TQ]], channel_multiplier=-1))
    ev_ones = None
    for t, v in [(ones_col16, 1.0), (ones_row32, 1.0), (ones_row16, 1.0),
                 (one11, 1.0), (kmax2, 0.0), (eps_c, 1e-5)]:
        ev_ones = DVE.op(lambda e, t=t, v=v: e.memset(t[:], v))

    stage_free = {("xin", 0): None, ("xin", 1): None, ("x1s", 0): None, ("x1s", 1): None}

    def plan_weight(w_h, w_sb, slab_name, slab):
        ev_last = None
        for half in range(2):
            st = slab[half]
            LD.wait(stage_free[(slab_name, half)])
            ev_ld = LD.op(lambda e, h=half, st=st, w_h=w_h: e.dma_start(
                out=st[:], in_=w_h.rearrange("(c p) d -> p c d", p=128)[:, 4 * h:4 * (h + 1), :]),
                dma=True)
            DVE.wait(ev_ld)
            ev_last = DVE.op(lambda e, h=half, st=st, w_sb=w_sb: e.tensor_copy(
                w_sb[:, h * 4096:(h + 1) * 4096], st[:]))
            stage_free[(slab_name, half)] = ev_last
        return ev_last

    plan_weight(wq_h, wq_sb, "xin", xin)
    ev_wk = plan_weight(wk_h, wk_sb, "xin", xin)
    plan_weight(wv_h, wv_sb, "x1s", x1s)
    ev_wo = plan_weight(wo_h, wo_sb, "x1s", x1s)

    for _e in ev_smalls:
        DVE.wait(_e)
    DVE.op(lambda e: e.reciprocal(sinv11[:], sin11[:]))
    DVE.op(lambda e: e.reciprocal(skinv11[:], skl11[:]))
    ev_pss = DVE.op(lambda e: e.tensor_tensor(out=pss11[:], in0=sin11[:], in1=skl11[:], op=ALU.mult))
    PE.wait(ev_pss)
    PE.wait(ev_ones)
    PE.op(lambda e: e.matmul(miscp[:, 0:1], ones_row32[:], sinv11[:], start=True, stop=True))
    PE.op(lambda e: e.matmul(miscp[:, 1:2], ones_row32[:], skinv11[:], start=True, stop=True))
    bc2 = PE.op(lambda e: e.matmul(miscp[:, 2:3], ones_row32[:], pss11[:], start=True, stop=True))
    DVE.wait(bc2)
    ev_sc = None
    for tgt, col in [(sinv_c, 0), (skinv_c, 1), (pss_c, 2)]:
        ev_sc = DVE.op(lambda e, t=tgt, c=col: e.tensor_copy(t[:], miscp[:, c:c + 1]))

    LD.wait(ev_wo)
    ev_r1 = LD.op(lambda e: e.dma_start(out=bvr, in_=bvr_h[:]), dma=True)
    ev_r2 = LD.op(lambda e: e.dma_start(out=bor, in_=bor_h[:]), dma=True)
    ev_r3 = LD.op(lambda e: e.dma_start(out=fbr, in_=fb_h[:]), dma=True)
    ev_bias_last = None
    PE.wait(ev_r1)
    PE.wait(ev_r2)
    PE.wait(ev_r3)
    PE.wait(ev_sc)
    for (row, bc, o16) in [(bvr, bv_bc, False), (bor, bo_bc, False), (fbr, fb_bc, True)]:
        for half in range(2):
            lhs = ones_row16 if o16 else ones_row32
            pv = PE.op(lambda e, row=row, h=half, lhs=lhs: e.matmul(
                miscp[:, 0:512], lhs[:], row[:, h * 512:(h + 1) * 512], start=True, stop=True))
            if o16:
                ACT.wait(pv)
                ev = ACT.op(lambda e, bc=bc, h=half: e.activation(
                    out=bc[:, h * 512:(h + 1) * 512], in_=miscp[:, 0:512], func=AF.Copy))
            else:
                DVE.wait(pv)
                ev = DVE.op(lambda e, bc=bc, h=half: e.tensor_copy(
                    bc[:, h * 512:(h + 1) * 512], miscp[:, 0:512]))
            PE.wait(ev)
            ev_bias_last = ev
    ev_x1s0_free = ev_bias_last

    ev_bq32 = DVE.op(lambda e: e.tensor_scalar(out=bqd[:], in0=bqd[:], scalar1=1.0 / 32.0,
                                               scalar2=None, op0=ALU.mult))

    ev_w8 = None
    for half in range(2):
        stb = xin[0].ap().bitcast(BF16)[:, 0:4096]
        LD.wait(ev_wk)
        LD.wait(ev_w8)
        ev_ld = LD.op(lambda e, h=half, stb=stb: e.dma_start(
            out=stb, in_=w1_h.rearrange("(c p) d -> p c d", p=128)[:, 4 * h:4 * (h + 1), :]),
            dma=True)
        ACT.wait(ev_ld)
        ACT.wait(ev_sc)
        ev_w8 = ACT.op(lambda e, h=half, stb=stb: e.activation(
            out=w8_sb[:, h * 4096:(h + 1) * 4096], in_=stb, func=AF.Identity,
            scale=skinv_c[:]))

    def plan_ln(src, dst16, bnb, stt, ev_src, ev_stats_free, ev_dst_free):
        DVE.wait(ev_src)
        DVE.wait(ev_stats_free)
        DVE.op(lambda e, src=src, bnb=bnb: e.bn_stats(bnb[:, 0:6], src[:, 0:512]))
        DVE.op(lambda e, src=src, bnb=bnb: e.bn_stats(bnb[:, 6:12], src[:, 512:1024]))
        ev_aggr = DVE.op(lambda e, bnb=bnb, stt=stt: e.bn_aggr(stt[:, 0:2], bnb[:]))
        ACT.wait(ev_aggr)
        ACT.op(lambda e, stt=stt: e.activation(out=stt[:, 2:3], in_=stt[:, 1:2],
                                               func=AF.Ln, bias=eps_c[:]))
        ev_rstd = ACT.op(lambda e, stt=stt: e.activation(out=stt[:, 3:4], in_=stt[:, 2:3],
                                                         func=AF.Exp, scale=-0.5))
        DVE.wait(ev_rstd)
        DVE.op(lambda e, stt=stt: e.tensor_scalar(out=stt[:, 4:5], in0=stt[:, 0:1],
                                                  scalar1=stt[:, 3:4], scalar2=-1.0,
                                                  op0=ALU.mult, op1=ALU.mult))
        DVE.wait(ev_dst_free)
        ev_dst = DVE.op(lambda e, src=src, dst16=dst16, stt=stt: e.tensor_scalar(
            out=dst16[:], in0=src[:], scalar1=stt[:, 3:4], scalar2=stt[:, 4:5],
            op0=ALU.mult, op1=ALU.add))
        return ev_dst

    ev_xin_free = [ev_w8, ev_wk]
    ev_x1s_free = [ev_x1s0_free, ev_wo]
    ev_qdm_free = None
    ev_kdm_free = None
    ev_x16a_free = [None, None]
    ev_x16b_free = [None, None]
    ev_stA_free = [None, None]
    ev_stB_free = [None, None]
    ev_misc_free2 = [ev_bias_last]
    ev_misc_free3 = [ev_bias_last]
    ev_x16a_free2 = [None, None]
    ev_stp_free = [None, None]
    ev_ctxpA_free = [None, None, None, None]
    ev_y16T_free = [None]
    ev_q8T_free = [None, None]
    ev_consts_waited = [False]
    ev_yslice_free = [None, None, None, None]
    knst = {"prev": None, "kmax": ev_ones, "qdm": None, "kdm": None,
            "vtm": None, "knmax": None, "lnq": None, "mall": [None] * 32}

    def wait_all(E, ev):
        if isinstance(ev, list):
            for _x in ev:
                E.wait(_x)
        else:
            E.wait(ev)

    lda = {}
    g_x16 = {}
    g_q8 = {}
    g_x1 = {}
    g_y16 = {}
    g_yT = {}
    g_v = {}

    def plan_lda(t):
        if t >= NT_A:
            return
        bi = t % 2
        LD.wait(ev_xin_free[bi])
        lda[t] = [LD.op(lambda e, t=t, u=u, bi=bi: e.dma_start(
            out=xin[bi][:, u * 1024:(u + 1) * 1024],
            in_=x_h[t * 512 + u * 128: t * 512 + (u + 1) * 128, :]), dma=True)
            for u in range(4)]

    def A1(sN):
        t, u = sN // 4, sN % 4
        bi = t % 2
        sp = sN % 2
        xa = xin[bi][:, u * 1024:(u + 1) * 1024]
        DVE.wait(ev_x16a_free2[sp])
        ACT.wait(ev_x16a_free2[sp])
        ev = plan_ln(xa, x16a[sp], bnbA[sp], statsA[sp],
                     lda[t][u], ev_stA_free[sp], ev_x16a_free[sp])
        ev_stA_free[sp] = ev
        g_x16[sN] = ev

    def A2(sN):
        sp = sN % 2
        PE.wait(g_x16[sN])
        PE.wait(ev_misc_free2[0])
        PE.wait(ev_misc_free3[0])
        if not ev_consts_waited[0]:
            PE.wait(ev_consts)
            ev_consts_waited[0] = True
        ev_tr = None
        for c in range(DC):
            ev_tr = PE.op(lambda e, c=c, sp=sp: e.transpose(
                miscp16[:, c * 128:(c + 1) * 128], x16a[sp][:, c * 128:(c + 1) * 128], ident[:]))
        ACT.wait(ev_tr)
        ACT.wait(ev_q8T_free[sp])
        q8dst = q8T.ap().rearrange("p (c t) -> p c t", c=DC)[:, :, sp * 128:(sp + 1) * 128]
        ev_q8 = ACT.op(lambda e, q8dst=q8dst: e.activation(
            out=q8dst, in_=miscp16[:, 0:1024], func=AF.Identity, scale=sinv_c[:]))
        ev_misc_free2[0] = ev_q8
        ev_misc_free3[0] = ev_q8
        ev_x16a_free[sp] = ev_q8
        ev_x16a_free2[sp] = ev_q8
        g_q8[sN] = ev_q8

    def A3(sN):
        t, u = sN // 4, sN % 4
        bi = t % 2
        sp = sN % 2
        PE.wait(g_q8[sN])
        PE.wait(ev_ctxpA_free[0])
        PE.wait(ev_ctxpA_free[1])
        ev_f = None
        for half in range(2):
            for c in range(DC):
                ev_f = PE.op(lambda e, c=c, sp=sp, half=half: e.matmul(
                    ctxp[:, half * 512:(half + 1) * 512],
                    q8T[:, c * 256 + sp * 128: c * 256 + (sp + 1) * 128],
                    w8_sb[:, c * 1024 + half * 512: c * 1024 + (half + 1) * 512],
                    start=(c == 0), stop=(c == DC - 1)))
        ev_q8T_free[sp] = ev_f
        DVE.wait(ev_f)
        ev_fa = DVE.op(lambda e: e.tensor_scalar(
            out=f16a[:], in0=ctxp[:, 0:1024], scalar1=pss_c[:], scalar2=None, op0=ALU.mult))
        ev_ctxpA_free[0] = ev_fa
        ev_ctxpA_free[1] = ev_fa
        DVE.op(lambda e: e.tensor_tensor(out=f16a[:], in0=f16a[:], in1=fb_bc[:], op=ALU.add))
        if u == 0:
            wait_all(DVE, ev_x1s_free[bi])
        xa = xin[bi][:, u * 1024:(u + 1) * 1024]
        x1a = x1s[bi][:, u * 1024:(u + 1) * 1024]
        g_x1[sN] = DVE.op(lambda e, xa=xa, x1a=x1a: e.scalar_tensor_tensor(
            out=x1a, in0=f16a[:], scalar=0.5, in1=xa, op0=ALU.mult, op1=ALU.add))

    def A4(sN):
        t, u = sN // 4, sN % 4
        bi = t % 2
        sp = sN % 2
        x1a = x1s[bi][:, u * 1024:(u + 1) * 1024]
        ev = plan_ln(x1a, x16b[sp], bnbB[sp], statsB[sp],
                     g_x1[sN], ev_stB_free[sp], ev_x16b_free[sp])
        ev_stB_free[sp] = ev
        g_y16[sN] = ev

    def A5(sN):
        t, u = sN // 4, sN % 4
        sp = sN % 2
        PE.wait(g_y16[sN])
        PE.wait(ev_misc_free2[0])
        PE.wait(ev_misc_free3[0])
        ev_tr = None
        for c in range(DC):
            ev_tr = PE.op(lambda e, c=c, sp=sp: e.transpose(
                miscp16[:, c * 128:(c + 1) * 128], x16b[sp][:, c * 128:(c + 1) * 128], ident[:]))
        if t > 0 and u == 0:
            DVE.wait(ev_y16T_free[0])
        DVE.wait(ev_yslice_free[u])
        DVE.wait(ev_tr)
        yTdst = y16T.ap().rearrange("p (c t) -> p c t", c=DC)[:, :, u * 128:(u + 1) * 128]
        ev_yT = DVE.op(lambda e, yTdst=yTdst: e.tensor_copy(yTdst, miscp16[:, 0:1024]))
        ev_misc_free2[0] = ev_yT
        ev_misc_free3[0] = ev_yT
        ev_x16b_free[sp] = ev_yT
        g_yT[sN] = ev_yT

    def A6(sN):
        t, u = sN // 4, sN % 4
        PE.wait(g_yT[sN])
        PE.wait(ev_ctxpA_free[2])
        PE.wait(ev_ctxpA_free[3])
        ev_v = None
        for half in range(2):
            for c in range(DC):
                ev_v = PE.op(lambda e, c=c, u=u, half=half: e.matmul(
                    ctxp[:, 1024 + half * 512: 1024 + (half + 1) * 512],
                    y16T[:, c * 512 + u * 128: c * 512 + (u + 1) * 128],
                    wv_sb[:, c * 1024 + half * 512: c * 1024 + (half + 1) * 512],
                    start=(c == 0), stop=(c == DC - 1)))
        ev_yslice_free[u] = ev_v
        DVE.wait(ev_v)
        if t > 0 and u == 0:
            wait_all(DVE, knst["vtm"])
        g_v[sN] = DVE.op(lambda e, u=u: e.scalar_tensor_tensor(
            out=vtm[:, u * 1024:(u + 1) * 1024], in0=ctxp[:, 1024:2048], scalar=1.0,
            in1=bv_bc[:], op0=ALU.mult, op1=ALU.add))
        ev_ctxpA_free[2] = g_v[sN]
        ev_ctxpA_free[3] = g_v[sN]

    def Pr(t):
        PE.wait(g_yT[4 * t + 3])
        DVE.wait(ev_bq32)
        ev_qevac = None
        ev_kevac = None
        ev_kn = None
        ev_mm = None
        ev_sq_l = [None] * DC
        ev_kn_l = [None] * DC
        ev_qn_last = None
        for qout in (True, False):
            wsb = wq_sb if qout else wk_sb
            for co in range(DC):
                pbank = stp[co % 2]
                PE.wait(ev_stp_free[co % 2])
                for c in range(DC):
                    ev_mm = PE.op(lambda e, c=c, co=co, wsb=wsb, pbank=pbank: e.matmul(
                        pbank[:], wsb[:, c * 1024 + co * 128: c * 1024 + co * 128 + 128],
                        y16T[:, c * 512:(c + 1) * 512],
                        start=(c == 0), stop=(c == DC - 1)))
                ACT.wait(ev_mm)
                if qout:
                    ACT.wait(knst["qdm"] if co == 0 else None)
                    ev_qevac = ACT.op(lambda e, co=co, pbank=pbank: e.activation(
                        out=qdm[:, co * 512:(co + 1) * 512], in_=pbank[:],
                        func=AF.Identity, scale=1.0 / 32.0, bias=bqd[:, co:co + 1]))
                    ev_stp_free[co % 2] = ev_qevac
                    src = qdm
                else:
                    ACT.wait(knst["kdm"] if co == 0 else None)
                    if co == 0:
                        ACT.wait(ev_qn_last)
                    ev_kevac = ACT.op(lambda e, co=co, pbank=pbank: e.activation(
                        out=kdm[:, co * 512:(co + 1) * 512], in_=pbank[:],
                        func=AF.Identity, bias=bkd[:, co:co + 1]))
                    ev_stp_free[co % 2] = ev_kevac
                    src = kdm
                if co >= 2:
                    PE.wait(ev_sq_l[co - 2])
                    if qout and co == 2:
                        PE.wait(knst["prev"])
                    if not qout and co == 2:
                        PE.wait(knst["lnq"])
                    ev_kn = PE.op(lambda e, co=co: e.matmul(
                        sump[0:1, 0:512], ones_col16[:], ksq[(co - 2) % 2][:],
                        start=(co == 2), stop=False))
                    ev_kn_l[co - 2] = ev_kn
                ev_evac = ev_qevac if qout else ev_kevac
                DVE.wait(ev_evac)
                if qout and co == 0:
                    DVE.wait(knst["prev"])
                if co >= 2:
                    DVE.wait(ev_kn_l[co - 2])
                ev_sq_l[co] = DVE.op(lambda e, co=co, src=src: e.tensor_tensor(
                    out=ksq[co % 2][:], in0=src[:, co * 512:(co + 1) * 512],
                    in1=src[:, co * 512:(co + 1) * 512], op=ALU.mult))
                if co == DC - 1:
                    for cc in (co - 1, co):
                        PE.wait(ev_sq_l[cc])
                        ev_kn = PE.op(lambda e, cc=cc, co=co: e.matmul(
                            sump[0:1, 0:512], ones_col16[:], ksq[cc % 2][:],
                            start=False, stop=(cc == co)))
                        ev_kn_l[cc] = ev_kn
            if qout:
                ev_qn_last = ev_kn
                ACT.wait(ev_kn)
                ev_lnq = ACT.op(lambda e: e.activation(out=lnqA[:], in_=sump[0:1, 0:512],
                                                       func=AF.Ln))
                knst["lnq"] = ev_lnq
                DVE.wait(ev_lnq)
                for h in range(2):
                    knst["mall"][2 * t + h] = DVE.op(lambda e, t=t, h=h: e.reduce_max(
                        mall[0:1, 2 * t + h: 2 * t + h + 1],
                        lnqA[0:1, h * 256:(h + 1) * 256], axis=AX.X))
                STQ.wait(ev_qevac)
                ev_stq = STQ.op(lambda e, t=t: e.dma_start(
                    out=qd[:, :, t * 512:(t + 1) * 512].rearrange("c p t -> p c t"),
                    in_=qdm[:]), dma=True)
                knst["qdm"] = ev_stq
        DVE.wait(ev_kn)
        DVE.wait(knst["kmax"])
        ev_knmax = DVE.op(lambda e: e.reduce_max(tmp11[:], sump[0:1, 0:512], axis=AX.X))
        knst["prev"] = ev_knmax
        knst["knmax"] = ev_knmax
        knst["kmax"] = DVE.op(lambda e: e.tensor_tensor(out=kmax2[:], in0=kmax2[:],
                                                        in1=tmp11[:], op=ALU.max))
        STQ.wait(ev_kevac)
        ev_stk = STQ.op(lambda e, t=t: e.dma_start(
            out=kd[:, :, t * 512:(t + 1) * 512].rearrange("c p t -> p c t"),
            in_=kdm[:]), dma=True)
        knst["kdm"] = ev_stk
        ev_y16T_free[0] = ev_mm

    plan_lda(0)
    plan_lda(1)
    for g in range(4 * NT_A + 8):
        s6 = g - 8
        if 0 <= s6 < 4 * NT_A:
            A6(s6)
            if s6 % 4 == 3:
                t6 = s6 // 4
                STQ.wait(g_v[s6])
                knst["vtm"] = [STQ.op(lambda e, t6=t6, h=h: e.dma_start(
                    out=vt[t6 * 512 + h * 256: t6 * 512 + (h + 1) * 256, :].rearrange(
                        "(kc p) d -> p kc d", p=128),
                    in_=vtm[:, h * 2048:(h + 1) * 2048]), dma=True) for h in range(2)]
        s5 = g - 6
        if 0 <= s5 < 4 * NT_A:
            A5(s5)
        s4 = g - 4
        if 0 <= s4 < 4 * NT_A:
            A4(s4)
        s3 = g - 3
        if 0 <= s3 < 4 * NT_A:
            A3(s3)
            if s3 % 4 == 3:
                t3 = s3 // 4
                bi3 = t3 % 2
                STQ.wait(g_x1[s3])
                ev_stx1 = [STQ.op(lambda e, t3=t3, bi3=bi3, h=h: e.dma_start(
                    out=x1d[t3 * 512 + h * 256: t3 * 512 + (h + 1) * 256, :].rearrange(
                        "(u p) d -> p u d", p=128),
                    in_=x1s[bi3][:, h * 2048:(h + 1) * 2048]), dma=True) for h in range(2)]
                ev_x1s_free[bi3] = ev_stx1
                ev_xin_free[bi3] = g_x1[s3]
                plan_lda(t3 + 2)
        s2 = g - 2
        if 0 <= s2 < 4 * NT_A:
            A2(s2)
        if g < 4 * NT_A:
            A1(g)
        s5n = g - 5
        if s5n % 4 == 0 and 4 <= s5n <= 4 * NT_A:
            Pr(s5n // 4 - 1)

    ev_qdm_free = knst["qdm"]
    ev_kdm_free = knst["kdm"]
    ev_knmax = knst["knmax"]
    ev_kmax_upd = knst["kmax"]

    ACT.wait(ev_kmax_upd)
    ev_lnk = ACT.op(lambda e: e.activation(out=lnk2h[:], in_=kmax2[:], func=AF.Ln))
    DVE.wait(ev_lnk)
    ev_lnk2 = DVE.op(lambda e: e.tensor_scalar(out=lnk2h[:], in0=lnk2h[:], scalar1=0.5,
                                               scalar2=None, op0=ALU.mult))

    for _i, _sem in enumerate(STQ.dma_sems):
        LD.wait(Ev(_sem, STQ.dma_counts[_i]))
    LD.wait(Ev(STQ.sem, STQ.count))

    ev_qtl_free = [ev_qdm_free, ev_qdm_free]
    ev_kv_free = [ev_xin_free[0], ev_xin_free[1], ev_y16T_free[0], ev_y16T_free[0]]
    ev_x1in_free = [ev_x1s_free[0], ev_x1s_free[1]]
    ev_outs_free = [ev_x1s_free[0], ev_x1s_free[1]]
    ev_stpB_free = list(ev_stp_free)
    ev_ctxp_free = list(ev_ctxpA_free)
    ev_sump_free = [ev_knmax]
    ev_qsqB_free = [ev_kdm_free]
    ev_ctx16_free = [ev_kdm_free, ev_kdm_free]
    ev_ctxT_free = [ev_y16T_free[0], ev_y16T_free[0]]
    ev_pt_free = [None, None, None, None]
    ev_ptt_free = [None]
    ev_x2_free = [knst["vtm"], knst["vtm"]]
    ev_negmcol_free = [None, None]
    ev_negmcol_ready = [None, None]
    ev_negm_prev = [None]
    ev_ldq_ev = [None, None]
    ev_ldx1_ev = [None, None]
    last_store = None

    def plan_loads(qi):
        if qi >= NQT:
            return
        bq_ = qi % 2
        LD.wait(ev_qtl_free[bq_])
        evq = [LD.op(lambda e, qi=qi, bq_=bq_, h=h: e.dma_start(
            out=qtl[bq_][:, h * 512:(h + 1) * 512],
            in_=qd[2 * h:2 * (h + 1), :, qi * TQ:(qi + 1) * TQ].rearrange("c p t -> p c t")),
            dma=True) for h in range(4)]
        ev_ldq_ev[bq_] = evq
        wait_all(LD, ev_x1in_free[bq_])
        evx = [LD.op(lambda e, qi=qi, bq_=bq_, h=h: e.dma_start(
            out=x1in[bq_][:, h * 1024:(h + 1) * 1024],
            in_=x1d[qi * TQ + h * 128: qi * TQ + (h + 1) * 128, :]), dma=True)
            for h in range(2)]
        ev_ldx1_ev[bq_] = evx

    def plan_prep(qi):
        bq_ = qi % 2
        for _ev in ev_ldq_ev[bq_]:
            PE.wait(_ev)
        ACT.wait(knst["mall"][qi])
        ACT.wait(ev_lnk2)
        ACT.wait(ev_negm_prev[0])
        ev_em = ACT.op(lambda e, qi=qi: e.activation(
            out=tmpM[:], in_=mall[0:1, qi:qi + 1], func=AF.Exp, scale=0.5, bias=lnk2h[:]))
        DVE.wait(ev_em)
        ev_negm = DVE.op(lambda e: e.tensor_scalar(
            out=negm[:], in0=tmpM[:], scalar1=-1.0, scalar2=None, op0=ALU.mult))
        ev_negm_prev[0] = ev_negm
        PE.wait(ev_negm)
        PE.wait(ev_misc_free2[0])
        PE.wait(ev_misc_free3[0])
        ev_bc = PE.op(lambda e: e.matmul(miscp[:, 0:1], ones_row32[:], negm[:], start=True, stop=True))
        ev_misc_free2[0] = ev_bc
        DVE.wait(ev_bc)
        DVE.wait(ev_negmcol_free[bq_])
        ev_negmcol_ready[bq_] = DVE.op(lambda e, bq_=bq_: e.tensor_copy(negmcol[bq_][:], miscp[:, 0:1]))

    slab_j = [None] * 4
    slab_ldk = [None] * 4
    slab_ldv = [None] * 4

    def ensure_loaded(j):
        bj = j % 4
        if slab_j[bj] == j:
            return
        LD.wait(ev_kv_free[bj])
        slab_ldk[bj] = [
            LD.op(lambda e, j=j, bj=bj, h=h: e.dma_start(
                out=kst[bj][:, h * 2048:(h + 1) * 2048],
                in_=kd[4 * h:4 * (h + 1), :, j * SUP:(j + 1) * SUP].rearrange("c p t -> p c t")),
                dma=True)
            for h in range(2)]
        slab_ldv[bj] = [
            LD.op(lambda e, j=j, bj=bj, h=h: e.dma_start(
                out=vst[bj][:, h * 2048:(h + 1) * 2048],
                in_=vt[j * SUP + h * 256: j * SUP + (h + 1) * 256, :].rearrange(
                    "(kc p) d -> p kc d", p=128)),
                dma=True)
            for h in range(2)]
        slab_j[bj] = j

    stpT16B = [stp[i].ap().bitcast(BF16) for i in range(2)]
    plan_loads(0)
    plan_prep(0)
    carryB = {}

    for qi in range(NQT):
        bq_ = qi % 2
        asc = (qi % 2 == 0)
        nch = (qi + 1) * 2
        nsup = (nch * 128 + SUP - 1) // SUP
        seq = list(range(nch)) if asc else list(range(nch - 1, -1, -1))
        sup_order = list(range(nsup)) if asc else list(range(nsup - 1, -1, -1))
        first_kg = seq[0]
        last_kg = seq[-1]

        state = {"ctx_last": None, "sum_last": None, "ready": None}
        ev_s_of = [None] * nch
        prep_done = [qi == NQT - 1]
        ev_x1bo = [None, None]

        def plan_S(kg, bq_=bq_, ev_s_of=ev_s_of):
            bj = (kg // 4) % 4
            kc = kg % 4
            sp = kg % 2
            pbank = stp[sp]
            for _ev in slab_ldk[bj]:
                PE.wait(_ev)
            PE.wait(ev_stpB_free[sp])
            ev = None
            for c in range(DC):
                ev = PE.op(lambda e, c=c, kc=kc, bj=bj, bq_=bq_, pbank=pbank: e.matmul(
                    pbank[:, 0:TQ],
                    kst[bj][:, c * 512 + kc * 128: c * 512 + (kc + 1) * 128],
                    qtl[bq_][:, c * TQ:(c + 1) * TQ],
                    start=(c == 0), stop=(c == DC - 1)))
            ev_s_of[kg] = ev

        def plan_ctx_sum(kg, ev_ready, first, last, state=state):
            bj = (kg // 4) % 4
            kc = kg % 4
            pti = kg % 4
            PE.wait(ev_ready)
            PE.wait(slab_ldv[bj][kc // 2])
            for u in range(2):
                if first:
                    PE.wait(ev_ctxp_free[2 * u])
                    PE.wait(ev_ctxp_free[2 * u + 1])
                for half in range(2):
                    state["ctx_last"] = PE.op(lambda e, kc=kc, u=u, half=half, bj=bj, pti=pti, first=first, last=last: e.matmul(
                        ctxp[:, u * 1024 + half * 512: u * 1024 + (half + 1) * 512],
                        pt[pti][:, u * 128:(u + 1) * 128],
                        vst[bj][:, kc * 1024 + half * 512: kc * 1024 + (half + 1) * 512],
                        start=first, stop=last))
            if first:
                PE.wait(ev_sump_free[0])
            state["sum_last"] = PE.op(lambda e, pti=pti, first=first, last=last: e.matmul(
                sump[0:1, 0:TQ], ones_col16[:], pt[pti][:, 0:TQ],
                start=first, stop=last))
            ev_pt_free[pti] = state["sum_last"]

        def plan_exp(kg, nch=nch, bq_=bq_, ev_s_of=ev_s_of, state=state):
            sp = kg % 2
            pti = kg % 4
            pbank = stp[sp]
            diag = (kg >= nch - 2)
            ACT.wait(ev_s_of[kg])
            ACT.wait(ev_negmcol_ready[bq_])
            if diag:
                ACT.wait(ev_ptt_free[0])
                ev_exp = ACT.op(lambda e, pbank=pbank, bq_=bq_: e.activation(
                    out=ptt[:], in_=pbank[:, 0:TQ], func=AF.Exp, bias=negmcol[bq_][:]))
                DVE.wait(ev_exp)
                DVE.wait(ev_pt_free[pti])
                m = maskA if (kg == nch - 2) else maskB
                ev_ready = DVE.op(lambda e, pti=pti, m=m: e.tensor_tensor(
                    out=pt[pti][:], in0=ptt[:], in1=m[:], op=ALU.mult))
                ev_ptt_free[0] = ev_ready
            else:
                ACT.wait(ev_pt_free[pti])
                ev_ready = ACT.op(lambda e, pti=pti, pbank=pbank, bq_=bq_: e.activation(
                    out=pt[pti][:], in_=pbank[:, 0:TQ], func=AF.Exp, bias=negmcol[bq_][:]))
            state["ready"] = ev_ready
            ev_stpB_free[sp] = ev_ready
            return ev_ready

        ensure_loaded(sup_order[0])
        if not prep_done[0]:
            plan_loads(qi + 1)
        for _a in (1, 2, 3):
            if nsup > _a:
                ensure_loaded(sup_order[_a])
        if first_kg not in carryB:
            plan_S(first_kg)
        for idx, kg in enumerate(seq):
            nxt = seq[idx + 1] if idx + 1 < nch else None
            if nxt is not None and nxt not in carryB:
                plan_S(nxt)
            if kg in carryB:
                ev_s_of[kg], _rdy = carryB.pop(kg)
                plan_ctx_sum(kg, _rdy, idx == 0, idx == nch - 1)
            else:
                _rdy = plan_exp(kg)
                plan_ctx_sum(kg, _rdy, idx == 0, idx == nch - 1)
            j = kg // 4
            sup_done = (kg % 4 == 3 or kg == nch - 1) if asc else (kg % 4 == 0)
            if sup_done:
                ev_kv_free[j % 4] = state["ctx_last"]
                pos = sup_order.index(j)
                for ahead in (2, 3):
                    if pos + ahead < nsup:
                        ensure_loaded(sup_order[pos + ahead])
            if idx == min(3, nch - 1):
                for _ev in ev_ldx1_ev[bq_]:
                    DVE.wait(_ev)
                for u in range(2):
                    ev_x1bo[u] = DVE.op(lambda e, u=u, bq_=bq_: e.tensor_tensor(
                        out=x1in[bq_][:, u * 1024:(u + 1) * 1024],
                        in0=x1in[bq_][:, u * 1024:(u + 1) * 1024], in1=bo_bc[:],
                        op=ALU.add))
                if not prep_done[0]:
                    plan_prep(qi + 1)
                    prep_done[0] = True
        if not prep_done[0]:
            plan_prep(qi + 1)
            prep_done[0] = True
        ev_qtl_free[bq_] = ev_s_of[last_kg]
        ev_negmcol_free[bq_] = state["ready"]
        ev_ctx_last = state["ctx_last"]
        ev_sum_last = state["sum_last"]

        DVE.wait(ev_sum_last)
        ev_rrow = DVE.op(lambda e: e.reciprocal(rrow[:], sump[0:1, 0:TQ]))
        ev_sump_free[0] = ev_rrow
        ACT.wait(ev_ctx_last)
        ACT.wait(ev_ctx16_free[0])
        ev_c16_0 = ACT.op(lambda e: e.activation(out=ctx16[0], in_=ctxp[:, 0:1024], func=AF.Copy))
        DVE.wait(ev_ctx_last)
        DVE.wait(ev_ctx16_free[1])
        ev_c16_1 = DVE.op(lambda e: e.tensor_copy(ctx16[1], ctxp[:, 1024:2048]))
        ev_c16 = [ev_c16_0, ev_c16_1]

        if qi + 1 < NQT:
            bq2 = (qi + 1) % 2
            nch2 = (qi + 2) * 2
            asc2 = not asc
            peel_kgs = [0, 1, 2, 3] if asc2 else [nch2 - 1, nch2 - 2, nch2 - 3, nch2 - 4]
            ensure_loaded(peel_kgs[0] // 4)
            for kg2 in peel_kgs:
                bj2 = (kg2 // 4) % 4
                kc2 = kg2 % 4
                sp2 = kg2 % 2
                pti2 = kg2 % 4
                pbank2 = stp[sp2]
                for _ev in slab_ldk[bj2]:
                    PE.wait(_ev)
                PE.wait(ev_stpB_free[sp2])
                ev_s2 = None
                for c in range(DC):
                    ev_s2 = PE.op(lambda e, c=c, kc2=kc2, bj2=bj2, bq2=bq2, pbank2=pbank2: e.matmul(
                        pbank2[:, 0:TQ],
                        kst[bj2][:, c * 512 + kc2 * 128: c * 512 + (kc2 + 1) * 128],
                        qtl[bq2][:, c * TQ:(c + 1) * TQ],
                        start=(c == 0), stop=(c == DC - 1)))
                diag2 = (kg2 >= nch2 - 2)
                ACT.wait(ev_s2)
                ACT.wait(ev_negmcol_ready[bq2])
                if diag2:
                    ACT.wait(ev_ptt_free[0])
                    ev_e2 = ACT.op(lambda e, pbank2=pbank2, bq2=bq2: e.activation(
                        out=ptt[:], in_=pbank2[:, 0:TQ], func=AF.Exp, bias=negmcol[bq2][:]))
                    DVE.wait(ev_e2)
                    DVE.wait(ev_pt_free[pti2])
                    m2 = maskA if (kg2 == nch2 - 2) else maskB
                    ev_r2 = DVE.op(lambda e, pti2=pti2, m2=m2: e.tensor_tensor(
                        out=pt[pti2][:], in0=ptt[:], in1=m2[:], op=ALU.mult))
                    ev_ptt_free[0] = ev_r2
                else:
                    ACT.wait(ev_pt_free[pti2])
                    ev_r2 = ACT.op(lambda e, pti2=pti2, pbank2=pbank2, bq2=bq2: e.activation(
                        out=pt[pti2][:], in_=pbank2[:, 0:TQ], func=AF.Exp, bias=negmcol[bq2][:]))
                ev_stpB_free[sp2] = ev_r2
                carryB[kg2] = (ev_s2, ev_r2)

        PE.wait(ev_rrow)
        PE.wait(ev_misc_free2[0])
        PE.wait(ev_misc_free3[0])
        PE.wait(ev_negmcol_ready[1 - bq_])
        ev_rc = []
        for u in range(2):
            ev_rc.append(PE.op(lambda e, u=u: e.matmul(
                miscp[:, 1 + u: 2 + u], rrow[:, u * 128:(u + 1) * 128], one11[:],
                start=True, stop=True)))
        DVE.wait(ev_rc[-1])
        ev_rcol = []
        for u in range(2):
            ev_rcol.append(DVE.op(lambda e, u=u: e.tensor_copy(rcol[u][:], miscp[:, 1 + u:2 + u])))
        ev_misc_free3[0] = ev_rcol[-1]
        ev_wo_l = [None, None]
        ev_trc = [None, None]
        for u in range(2):
            PE.wait(ev_c16[u])
            PE.wait(ev_stpB_free[u])
            ev_tr = None
            for c in range(DC):
                ev_tr = PE.op(lambda e, u=u, c=c: e.transpose(
                    stpT16B[u][:, c * 128:(c + 1) * 128], ctx16[u][:, c * 128:(c + 1) * 128], ident[:]))
            ev_trc[u] = ev_tr
            if u == 0:
                ACT.wait(ev_tr)
                ACT.wait(ev_ctxT_free[u])
                evc = ACT.op(lambda e, u=u: e.activation(
                    out=ctxT[u], in_=stpT16B[u][:, 0:1024], func=AF.Identity))
            else:
                DVE.wait(ev_tr)
                DVE.wait(ev_ctxT_free[u])
                evc = DVE.op(lambda e, u=u: e.tensor_copy(ctxT[u], stpT16B[u][:, 0:1024]))
            ev_stpB_free[u] = evc
            PE.wait(evc)
            for half in range(2):
                for c in range(DC):
                    ev_wo_l[u] = PE.op(lambda e, u=u, half=half, c=c: e.matmul(
                        ctxp[:, u * 1024 + half * 512: u * 1024 + (half + 1) * 512],
                        ctxT[u][:, c * 128:(c + 1) * 128],
                        wo_sb[:, c * 1024 + half * 512: c * 1024 + (half + 1) * 512],
                        start=(c == 0), stop=(c == DC - 1)))
            ev_ctx16_free[u] = ev_trc[u]
            ev_ctxT_free[u] = ev_wo_l[u]
        ev_x2 = [None, None]
        for u in range(2):
            ACT.wait(ev_wo_l[u])
            ACT.wait(ev_rcol[u])
            wait_all(ACT, ev_x2_free[u])
            ev_ev = ACT.op(lambda e, u=u: e.activation(
                out=x2f[u], in_=ctxp[:, u * 1024:(u + 1) * 1024],
                func=AF.Identity, scale=rcol[u][:]))
            ev_ctxp_free[2 * u] = ev_ev
            ev_ctxp_free[2 * u + 1] = ev_ev
            DVE.wait(ev_ev)
            DVE.wait(ev_x1bo[u])
            ev_x2[u] = DVE.op(lambda e, u=u, bq_=bq_: e.tensor_tensor(
                out=x2f[u], in0=x2f[u], in1=x1in[bq_][:, u * 1024:(u + 1) * 1024], op=ALU.add))
        ev_x1in_free[bq_] = ev_x2[1]

        STQ.wait(ev_x2[0])
        STQ.wait(ev_x2[1])
        ev_stx2 = STQ.op(lambda e, qi=qi: e.dma_start(
            out=x1d[qi * TQ:(qi + 1) * TQ, :].rearrange("(u p) d -> p u d", p=128),
            in_=vtm.ap().bitcast(F32)[:, 0:2048]), dma=True)
        ev_x2_free[0] = ev_stx2
        ev_x2_free[1] = ev_stx2

    for _i, _sem in enumerate(STQ.dma_sems):
        LD.wait(Ev(_sem, STQ.dma_counts[_i]))

    NSUB = NT_A * 4
    ev_xinC_free = [ev_kv_free[0], ev_kv_free[1]]
    def as_list(ev):
        return ev if isinstance(ev, list) else [ev]

    ev_outC_free = [as_list(ev_x1in_free[0]) + as_list(ev_outs_free[0]),
                    as_list(ev_x1in_free[1]) + as_list(ev_outs_free[1])]
    ev_q8TC_free = [Ev(PE.sem, PE.count), Ev(PE.sem, PE.count)]
    ev_ctxpC_free = [[ev_ctxp_free[0], ev_ctxp_free[1]],
                     [ev_ctxp_free[2], ev_ctxp_free[3]]]
    fC = [wv_sb[:, 0:1024], wv_sb[:, 1024:2048]]
    ev_fC_free = [Ev(DVE.sem, DVE.count), Ev(DVE.sem, DVE.count)]
    stpT16 = [stp[i].ap().bitcast(BF16) for i in range(2)]
    ev_stpC_free = [ev_stpB_free[0], ev_stpB_free[1]]
    ldC = {}
    st_l3g = {}
    st_q8g = {}
    st_outg = {}
    last_store = None

    def planC_loads(t):
        if t >= NT_A:
            return
        bi = t % 2
        LD.wait(ev_xinC_free[bi])
        ldC[t] = [LD.op(lambda e, t=t, u=u, bi=bi: e.dma_start(
            out=xin[bi][:, u * 1024:(u + 1) * 1024],
            in_=x1d[t * 512 + u * 128: t * 512 + (u + 1) * 128, :]), dma=True)
            for u in range(4)]

    def C1(sN):
        t, u = sN // 4, sN % 4
        bi = t % 2
        sp = sN % 2
        xa = xin[bi][:, u * 1024:(u + 1) * 1024]
        ev = plan_ln(xa, x16a[sp], bnbA[sp], statsA[sp],
                     ldC[t][u], ev_stA_free[sp], ev_x16a_free[sp])
        ev_stA_free[sp] = ev
        st_l3g[sN] = ev

    def C2(sN):
        sp = sN % 2
        PE.wait(st_l3g[sN])
        PE.wait(ev_stpC_free[sp])
        ev_tr = None
        for c in range(DC):
            ev_tr = PE.op(lambda e, c=c, sp=sp: e.transpose(
                stpT16[sp][:, c * 128:(c + 1) * 128], x16a[sp][:, c * 128:(c + 1) * 128], ident[:]))
        ACT.wait(ev_tr)
        ACT.wait(ev_q8TC_free[sp])
        q8dst = q8T.ap().rearrange("p (c t) -> p c t", c=DC)[:, :, sp * 128:(sp + 1) * 128]
        ev_q8 = ACT.op(lambda e, q8dst=q8dst, sp=sp: e.activation(
            out=q8dst, in_=stpT16[sp][:, 0:1024], func=AF.Identity, scale=sinv_c[:]))
        ev_stpC_free[sp] = ev_q8
        ev_x16a_free[sp] = ev_q8
        ev_x16a_free2[sp] = ev_q8
        st_q8g[sN] = ev_q8

    def C3(sN):
        t, u = sN // 4, sN % 4
        bi = t % 2
        sp = sN % 2
        pcb = ctxp[:, sp * 1024:(sp + 1) * 1024]
        PE.wait(st_q8g[sN])
        for _ev in ev_ctxpC_free[sp]:
            PE.wait(_ev)
        ev_f = None
        for half in range(2):
            for c in range(DC):
                ev_f = PE.op(lambda e, c=c, sp=sp, half=half, pcb=pcb: e.matmul(
                    pcb[:, half * 512:(half + 1) * 512],
                    q8T[:, c * 256 + sp * 128: c * 256 + (sp + 1) * 128],
                    w8_sb[:, c * 1024 + half * 512: c * 1024 + (half + 1) * 512],
                    start=(c == 0), stop=(c == DC - 1)))
        ev_q8TC_free[sp] = ev_f
        ACT.wait(ev_f)
        ACT.wait(ev_fC_free[sp])
        ev_ev = ACT.op(lambda e, sp=sp, pcb=pcb: e.activation(
            out=fC[sp][:], in_=pcb[:], func=AF.Identity, scale=pss_c[:]))
        ev_ctxpC_free[sp] = [ev_ev]
        DVE.wait(ev_ev)
        DVE.op(lambda e, sp=sp: e.tensor_tensor(out=fC[sp][:], in0=fC[sp][:],
                                                in1=fb_bc[:], op=ALU.add))
        for evx in ev_outC_free[bi]:
            DVE.wait(evx)
        xa = xin[bi][:, u * 1024:(u + 1) * 1024]
        oa = x1s[bi][:, u * 1024:(u + 1) * 1024]
        ev_o = DVE.op(lambda e, sp=sp, xa=xa, oa=oa: e.scalar_tensor_tensor(
            out=oa, in0=fC[sp][:], scalar=0.5, in1=xa, op0=ALU.mult, op1=ALU.add))
        ev_fC_free[sp] = ev_o
        st_outg[sN] = ev_o

    planC_loads(0)
    planC_loads(1)
    for g in range(NSUB + 4):
        if 0 <= g - 4 < NSUB:
            sc = g - 4
            C3(sc)
            if sc % 4 == 3:
                t = sc // 4
                bi = t % 2
                STQ.wait(st_outg[sc])
                ev_sto = STQ.op(lambda e, t=t, bi=bi: e.dma_start(
                    out=out_h[t * 512:(t + 1) * 512, :].rearrange("(u p) d -> p u d", p=128),
                    in_=x1s[bi][:]), dma=True)
                ev_outC_free[bi] = [ev_sto]
                ev_xinC_free[bi] = st_outg[sc]
                last_store = ev_sto
                planC_loads(t + 2)
        if 0 <= g - 2 < NSUB:
            C2(g - 2)
        if g < NSUB:
            C1(g)

    for _i, _sem in enumerate(STQ.dma_sems):
        STQ.raw(lambda e, s_=_sem, v_=STQ.dma_counts[_i]: e.wait_ge(s_, v_))
    s.emit()
    return nc


_NC_CACHE = {}


def kernel(L0_attn_y_all, L0_attn_Wq, L0_attn_bq, L0_attn_Wk, L0_attn_bk,
           L0_attn_Wv, L0_attn_bv, L0_attn_Wo, L0_attn_bo,
           L0_ffn_kernel, L0_ffn_input_scale, L0_ffn_kernel_scale, L0_ffn_bias):
    x = np.asarray(L0_attn_y_all)
    B = x.shape[0]
    assert x.shape == (8, T, D)

    def lay_bias_d(b):
        return np.ascontiguousarray(np.asarray(b, dtype=np.float32).reshape(DC, 128).T)

    common = {
        "L0_attn_Wq": np.ascontiguousarray(np.asarray(L0_attn_Wq, dtype=np.float32)),
        "L0_attn_Wk": np.ascontiguousarray(np.asarray(L0_attn_Wk, dtype=np.float32)),
        "L0_attn_Wv": np.ascontiguousarray(np.asarray(L0_attn_Wv, dtype=np.float32)),
        "L0_attn_Wo": np.ascontiguousarray(np.asarray(L0_attn_Wo, dtype=np.float32)),
        "L0_attn_bq": lay_bias_d(L0_attn_bq),
        "L0_attn_bk": lay_bias_d(L0_attn_bk),
        "L0_attn_bv": np.ascontiguousarray(np.asarray(L0_attn_bv, dtype=np.float32).reshape(1, D)),
        "L0_attn_bo": np.ascontiguousarray(np.asarray(L0_attn_bo, dtype=np.float32).reshape(1, D)),
        "L0_ffn_kernel": np.ascontiguousarray(np.asarray(L0_ffn_kernel).reshape(D, D)),
        "L0_ffn_input_scale": np.ascontiguousarray(np.asarray(L0_ffn_input_scale, dtype=np.float32).reshape(1, 1)),
        "L0_ffn_kernel_scale": np.ascontiguousarray(np.asarray(L0_ffn_kernel_scale, dtype=np.float32).reshape(1, 1)),
        "L0_ffn_bias": np.ascontiguousarray(np.asarray(L0_ffn_bias).reshape(1, D)),
    }
    in_maps = [dict(common, L0_attn_y_all=np.ascontiguousarray(x[b])) for b in range(B)]

    if "nc" not in _NC_CACHE:
        _NC_CACHE["nc"] = build_program()
    nc = _NC_CACHE["nc"]
    res = run_bass_kernel_spmd(nc, in_maps, core_ids=list(range(8)))
    out = np.stack([np.asarray(res.results[b]["out"]) for b in range(B)], axis=0)
    return out.astype(np.float32, copy=False)
